# version 03
# baseline (speedup 1.0000x reference)
import numpy as np
import ml_dtypes
import concourse.bass as bass
import concourse.mybir as mybir
from concourse.bass_utils import run_bass_kernel_spmd

F32 = mybir.dt.float32
BF16 = mybir.dt.bfloat16
AF = mybir.ActivationFunctionType
ALU = mybir.AluOpType
AX = mybir.AxisListType
bf16 = ml_dtypes.bfloat16

DBG = False
BUILD_STAGE = 3
ASTAGE = 99

T_TOT = 4096
C = 1024
TT = 128
NT = T_TOT // TT
TB = 256
NTB = T_TOT // TB
NH = 16
HD = 64
LCH = 64
NE = 8
HID = 4096


class _Op:
    __slots__ = ("eng", "fn", "deps", "inc", "val", "dma", "waits", "tag")

    def __init__(self, eng, fn, dma, tag):
        self.eng, self.fn, self.dma, self.tag = eng, fn, dma, tag
        self.deps = []
        self.inc = False
        self.val = 0
        self.waits = []


class KB:
    ENGS = ("tensor", "vector", "scalar", "gpsimd", "sync")

    def __init__(self, nc):
        self.nc = nc
        self.ops = []
        self.last_w = {}
        self.readers = {}
        self._fence_deps = []
        self._dma_rr = {"qsync": 0, "qpool": 0}
        self._last_dma = {}

    def op(self, eng, fn, reads=(), writes=(), tag=""):
        return self._add(eng, fn, reads, writes, None, tag)

    def dma(self, eng, fn, reads=(), writes=(), tag=""):
        q = "qpool" if eng == "gpsimd" else "qsync"
        slot = self._dma_rr[q] % 8
        self._dma_rr[q] += 1
        dom = f"{q}{slot}"
        idx = self._add(eng, fn, reads, writes, dom, tag)
        prev = self._last_dma.get(dom)
        if prev is not None and prev not in self.ops[idx].deps:
            self.ops[idx].deps.append(prev)
            self.ops[idx].deps.sort()
        self._last_dma[dom] = idx
        return idx

    def fence(self):
        last_by_dom = {}
        for i, op in enumerate(self.ops):
            last_by_dom[op.dma if op.dma else op.eng] = i
        self._fence_deps = sorted(last_by_dom.values())

    def _add(self, eng, fn, reads, writes, dma, tag):
        op = _Op(eng, fn, dma, tag)
        idx = len(self.ops)
        deps = set(self._fence_deps)
        for b in reads:
            t = self.last_w.get(b)
            if t is not None:
                deps.add(t)
        for b in writes:
            t = self.last_w.get(b)
            if t is not None:
                deps.add(t)
            for t2 in self.readers.get(b, ()):
                deps.add(t2)
        deps.discard(idx)
        op.deps = sorted(deps)
        self.ops.append(op)
        for b in reads:
            self.readers.setdefault(b, []).append(idx)
        for b in writes:
            self.last_w[b] = idx
            self.readers[b] = []
        return idx

    def finalize_and_emit(self):
        nc = self.nc
        for op in self.ops:
            for dd in op.deps:
                self.ops[dd].inc = True
        for op in self.ops:
            if op.dma:
                op.inc = True
        counters = {}
        for op in self.ops:
            if op.inc:
                dom = op.dma if op.dma else op.eng
                counters[dom] = counters.get(dom, 0) + (16 if op.dma else 1)
                op.val = counters[dom]
        for op in self.ops:
            w = {}
            for dd in op.deps:
                p = self.ops[dd]
                dom = p.dma if p.dma else p.eng
                w[dom] = max(w.get(dom, 0), p.val)
            op.waits = w
        hw = {e: {} for e in self.ENGS}
        for op in self.ops:
            pruned = {}
            h = hw[op.eng]
            for dom, v in op.waits.items():
                if h.get(dom, 0) < v:
                    pruned[dom] = v
                    h[dom] = v
            op.waits = sorted(pruned.items())
        sems = {}
        for dom in set(counters) | set(self.ENGS):
            sems[dom] = nc.alloc_semaphore(f"sem_{dom}")
        per_eng = {e: [op for op in self.ops if op.eng == e] for e in self.ENGS}
        final_waits = sorted(counters.items())

        with nc.allow_low_precision(reason="deliberate bf16 operands; fp32 accumulation on all matmuls"), nc.Block() as block:
            def make_body(engname, ops):
                def body(eng):
                    for op in ops:
                        for dom, v in op.waits:
                            eng.wait_ge(sems[dom], v)
                        ins = op.fn(eng)
                        if op.inc:
                            assert ins is not None, f"op {op.tag} returned None"
                            dom = op.dma if op.dma else op.eng
                            ins.then_inc(sems[dom], 16 if op.dma else 1)
                    if engname == "sync":
                        for dom, v in final_waits:
                            eng.wait_ge(sems[dom], v)
                return body
            for e in self.ENGS:
                getattr(block, e)(make_body(e, per_eng[e]))


def host_consts():
    ID_BF = np.eye(128, dtype=bf16)
    ID_F32 = np.eye(128, dtype=np.float32)
    tl = np.tril(np.ones((LCH, LCH), np.float32), -1)
    tu = np.triu(np.ones((LCH, LCH), np.float32), 1)
    ti = np.triu(np.ones((LCH, LCH), np.float32), 0)
    z = np.zeros_like(tl)
    bd = lambda m: np.block([[m, z], [z, m]]).astype(np.float32)
    MASKS = np.concatenate([bd(tl), bd(tu), bd(ti)], axis=1).astype(np.uint8)
    E8 = np.zeros((8, 8 * 128), dtype=bf16)
    for e in range(8):
        E8[e, e * 128:(e + 1) * 128] = 1
    EXP2 = np.zeros((2, 128), dtype=bf16)
    EXP2[0, 0:64] = 1
    EXP2[1, 64:128] = 1
    ONES2 = np.zeros((128, 2), dtype=bf16)
    ONES2[0:64, 0] = 1
    ONES2[64:128, 1] = 1
    ONES1 = np.ones((128, 1), np.float32)
    return {"c_idbf": ID_BF, "c_idf32": ID_F32, "c_masks": MASKS,
            "c_e8": E8, "c_exp2": EXP2, "c_ones2": ONES2, "c_ones1": ONES1,
            "c_misc": np.concatenate([np.ones((128, 1), np.float32),
                                      np.full((128, 1), 1e-5, np.float32)], axis=1)}


VEC_NAMES = ["x_r", "x_w", "x_k", "x_v", "x_a", "k_k", "k_a",
             "w_lora_bias", "a_lora_bias", "g_lora_bias", "gn_weight", "gn_bias"]
VIX = {n: i for i, n in enumerate(VEC_NAMES)}


def build_program():
    nc = bass.Bass("TRN2", target_bir_lowering=False)
    kb = KB(nc)
    o = kb.op
    d = kb.dma

    EI = lambda n, s, dt=F32: nc.dram_tensor(n, s, dt, kind="ExternalInput").ap()
    x_in = EI("x", [T_TOT, C])
    out_t = nc.dram_tensor("out", [T_TOT, C], F32, kind="ExternalOutput").ap()
    vecs_in = EI("vecs", [128, 8, 12])
    rk_in = EI("r_k", [128, 8])
    W_r = EI("W_r", [C, C]); W_k = EI("W_k", [C, C])
    W_v = EI("W_v", [C, C]); W_o = EI("W_o", [C, C])
    wla_in = EI("w_lora_a", [C, 64]); wlb_in = EI("w_lora_b", [64, C])
    ala_in = EI("a_lora_a", [C, 64]); alb_in = EI("a_lora_b", [64, C])
    gla_in = EI("g_lora_a", [C, 128]); glb_in = EI("g_lora_b", [128, C])
    gw_in = EI("gate_w", [8, C])
    W1_in = EI("W1", [NE, HID, C]); b1_in = EI("b1", [NE, 128, 32])
    W2_in = EI("W2", [NE, C, HID]); b2_in = EI("b2", [NE, 128, 8])
    c_idbf = EI("c_idbf", [128, 128], BF16)
    c_idf32 = EI("c_idf32", [128, 128], F32)
    c_masks = EI("c_masks", [128, 384], mybir.dt.uint8)
    c_e8 = EI("c_e8", [8, 8 * 128], BF16)
    c_exp2 = EI("c_exp2", [2, 128], BF16)
    c_ones2 = EI("c_ones2", [128, 2], BF16)
    c_ones1 = EI("c_ones1", [128, 1], F32)
    c_misc = EI("c_misc", [128, 2], F32)

    x2s = nc.dram_tensor("x2s", [T_TOT, C], F32, kind="Internal").ap()
    ys = nc.dram_tensor("ys", [NT, C, TT], BF16, kind="Internal").ap()
    ws = nc.dram_tensor("ws", [8, NT * TT], F32, kind="Internal").ap()
    accs = nc.dram_tensor("accs", [NTB, C, TB], F32, kind="Internal").ap()
    w1bf = nc.dram_tensor("w1bf", [HID, C], BF16, kind="Internal").ap()
    w2bf = nc.dram_tensor("w2bf", [C, HID], BF16, kind="Internal").ap()

    dbg = {}
    if DBG:
        for n, shape in [("d_hs", [C, T_TOT]), ("d_r", [C, T_TOT]),
                         ("d_logw", [C, T_TOT]), ("d_k2", [C, T_TOT]),
                         ("d_a", [C, T_TOT]), ("d_kk", [C, T_TOT]),
                         ("d_g", [C, T_TOT]), ("d_v", [C, T_TOT]),
                         ("d_o", [T_TOT, C]), ("d_of", [C, T_TOT]),
                         ("d_x2", [T_TOT, C]), ("d_y", [C, T_TOT]),
                         ("d_wf", [8, T_TOT])]:
            dbg[n] = nc.dram_tensor(n, shape, F32, kind="ExternalOutput").ap()

    sb = lambda name, shape, dt: nc.alloc_sbuf_tensor(name, shape, dt).ap()

    IDB = sb("IDB", [128, 128], BF16)
    IDF = sb("IDF", [128, 128], F32)
    MSK = sb("MSK", [128, 384], mybir.dt.uint8)
    E8S = sb("E8S", [8, 8 * 128], BF16)
    EX2 = sb("EX2", [2, 128], BF16)
    ON2 = sb("ON2", [128, 2], BF16)
    ON1 = sb("ON1", [128, 1], F32)
    MISC = sb("MISC", [128, 2], F32)
    ZRO = sb("ZRO", [128, 1], F32)
    VEC = sb("VEC", [128, 8, 12], F32)
    NEG = sb("NEG", [128, 8, 6], F32)
    rkv = sb("rkv", [128, 8], F32)
    rk2 = sb("rk2", [128, 8, 2], BF16)

    arena_base = nc.sbuf_base

    Wr = sb("Wr", [128, 8, C], BF16); Wk = sb("Wk", [128, 8, C], BF16)
    Wv = sb("Wv", [128, 8, C], BF16); Wo = sb("Wo", [128, 8, C], BF16)
    wla = sb("wla", [128, 8, 64], BF16); wlb = sb("wlb", [64, C], BF16)
    ala = sb("ala", [128, 8, 64], BF16); alb = sb("alb", [64, C], BF16)
    gla = sb("gla", [128, 8, 128], BF16); glb = sb("glb", [128, C], BF16)
    gwn = sb("gwn", [8, C], BF16)
    gwT = sb("gwT", [128, 8, 8], BF16)
    XT = sb("XT", [128, 2, C], F32)
    DUMP = sb("DUMP", [128, C], F32)
    STT = sb("STT", [128, 8], F32)
    XHC = sb("XHC", [128, C], BF16)
    XH_FM = sb("XH_FM", [128, 8, TT + 2], BF16)
    TMPM = sb("TMPM", [128, 8, TT], BF16)
    MIX = sb("MIX", [128, 2, 8, TT], BF16)
    R_FM = sb("R_FM", [128, 8, TT], BF16)
    K_FM = sb("K_FM", [128, 8, TT], BF16)
    V_FM = sb("V_FM", [128, 8, TT], BF16)
    KKR = sb("KKR", [128, 8, TT], BF16)
    KK_FM = sb("KK_FM", [128, 8, TT], BF16)
    A_FM = sb("A_FM", [128, 8, TT], BF16)
    K2_FM = sb("K2_FM", [128, 8, TT], BF16)
    SCR = sb("SCR", [128, 8, TT], F32)
    SC2 = sb("SC2", [2, 3, TT], F32)
    SCB = sb("SCB", [128, 8, TT], BF16)
    LOGW = sb("LOGW", [128, 8, TT], F32)
    CG = sb("CG", [128, 8, TT + 1], F32)
    CB = sb("CB", [128, 8, 3], F32)
    PMID = sb("PMID", [128, 8, 2], F32)
    ECL = sb("ECL", [128, 8, 2], F32)
    EPHI = sb("EPHI", [128, 8, 2], F32)
    FRT = sb("FRT", [128, 8, TT], BF16)
    FKB = sb("FKB", [128, 8, TT], BF16)
    FSH = sb("FSH", [128, 8, TT], BF16)
    SAR = sb("SAR", [128, 8, 2, TT], BF16)
    SB_ = sb("SB_", [128, 8, TT], BF16)
    SK_ = sb("SK_", [128, 8, TT], BF16)
    SAH = sb("SAH", [128, 8, TT], BF16)
    SRH = sb("SRH", [128, 8, TT], BF16)
    TMX = sb("TMX", [128, 2, 192], BF16)
    YST = sb("YST", [128, 2, 2, 128], BF16)
    MPL = sb("MPL", [128, 2, 2, 128], BF16)
    MTR = sb("MTR", [128, 2, 2, 128], BF16)
    CTM = sb("CTM", [128, 2, 128], BF16)
    BRU = sb("BRU", [128, 2, 128], BF16)
    KRU = sb("KRU", [128, 2, 128], BF16)
    ZTP = sb("ZTP", [128, 4, 128], BF16)
    WPD = sb("WPD", [128, 4, 64], BF16)
    YLC = sb("YLC", [128, 2, 64], F32)
    SST = sb("SST", [128, 8, 2, HD], BF16)
    STMP = sb("STMP", [128, 64], F32)
    OTM = sb("OTM", [128, C], F32)
    MU = sb("MU", [128, NH], F32)
    SSQ = sb("SSQ", [128, NH], F32)
    RSTD = sb("RSTD", [128, NH], F32)
    MRS = sb("MRS", [128, NH], F32)
    ON_FM = sb("ON_FM", [128, 8, TT], BF16)
    SBC = sb("SBC", [2, 8, TT], BF16)
    OG = sb("OG", [128, 8, TT], BF16)
    Y_FM = sb("Y_FM", [128, 8, TT], BF16)
    RT = sb("RT", [128, 8], F32)
    TOP = sb("TOP", [128, 40], F32)
    WFT = sb("WFT", [128, 8], F32)
    WF8 = sb("WF8", [8, TT], F32)
    X2T = OTM
    XH_TM = XHC
    ON_TM = XHC
    Y_TM = XHC
    G_FM = KKR
    PRK = K_FM

    assert nc.sbuf_base < nc.sbuf_top, "phase A SBUF overflow"

    boff = [arena_base]

    def sbB(name, shape, dt):
        per = int(np.prod(shape[1:])) * mybir.dt.size(dt)
        per = (per + 31) // 32 * 32
        h = nc.alloc_sbuf_tensor_at(name, shape, dt, offset=boff[0])
        boff[0] += per
        assert boff[0] <= nc.sbuf_top, f"phase B overflow at {name}"
        return h.ap()

    W1T = sbB("W1T", [128, 8, HID], BF16)
    W2T = sbB("W2T", [128, 32, C], BF16)
    HT = sbB("HT", [128, 32, TB], BF16)
    YTL = sbB("YTL", [128, 2, 8, TB], BF16)
    ACCN = sbB("ACCN", [128, 2, 8, TB], F32)
    ACC1 = sbB("ACC1", [128, 2, TB], F32)
    X2L = sbB("X2L", [128, 2, C], F32)
    OUTST = sbB("OUTST", [128, 2, C], F32)
    W8 = sbB("W8", [8, NT * TT], BF16)
    WB = sbB("WB", [128, TB], F32)
    B1C = sbB("B1C", [128, 32], F32)
    B2C = sbB("B2C", [128, 8], F32)

    psum = lambda name, shape, dt=F32: nc.alloc_psum_tensor(name, shape, dt).ap()
    PS0 = psum("PS0", [128, 512])
    PS1 = psum("PS1", [128, 512])
    PS2 = psum("PS2", [128, 512])
    PS3 = psum("PS3", [128, 512])
    PS4 = psum("PS4", [128, 512])
    PS5 = psum("PS5", [128, 512])
    PS6 = psum("PS6", [128, 512])
    PSB = psum("PSB", [128, 1024], BF16)

    maskA = MSK[:, 0:128]
    maskB = MSK[:, 128:256]
    maskC = MSK[:, 256:384]
    Z128 = ZRO.broadcast_to([128, 128])

    def vcol(n, cc):
        return VEC[:, cc, VIX[n]:VIX[n] + 1]

    d("sync", lambda e: e.dma_start(out=IDB, in_=c_idbf), writes=["IDB"])
    d("sync", lambda e: e.dma_start(out=IDF, in_=c_idf32), writes=["IDF"])
    d("sync", lambda e: e.dma_start(out=MSK, in_=c_masks), writes=["MSK"])
    d("sync", lambda e: e.dma_start(out=E8S, in_=c_e8), writes=["E8S"])
    d("sync", lambda e: e.dma_start(out=EX2, in_=c_exp2), writes=["EX2"])
    d("sync", lambda e: e.dma_start(out=ON2, in_=c_ones2), writes=["ON2"])
    d("sync", lambda e: e.dma_start(out=ON1, in_=c_ones1), writes=["ON1"])
    d("sync", lambda e: e.dma_start(out=MISC, in_=c_misc), writes=["MISC"])
    nc.const_aps.register(F32, 1.0, MISC[:, 0:1])
    nc.const_aps.register(F32, 1e-5, MISC[:, 1:2])
    d("sync", lambda e: e.dma_start(out=VEC, in_=vecs_in),
      writes=[("VEC", i) for i in range(12)])
    d("sync", lambda e: e.dma_start(out=rkv, in_=rk_in), writes=["rkv"])
    for nm, dst, src in [("Wr", Wr, W_r), ("Wk", Wk, W_k), ("Wv", Wv, W_v),
                         ("Wo", Wo, W_o), ("wla", wla, wla_in),
                         ("ala", ala, ala_in), ("gla", gla, gla_in)]:
        ap = src.rearrange("(kc p) n -> p kc n", p=128)
        d("gpsimd", lambda e, dst=dst, ap=ap: e.dma_start(out=dst, in_=ap),
          writes=[nm])
    for nm, dst, src in [("wlb", wlb, wlb_in), ("alb", alb, alb_in),
                         ("glb", glb, glb_in), ("gwn", gwn, gw_in)]:
        d("gpsimd", lambda e, dst=dst, src=src: e.dma_start(out=dst, in_=src),
          writes=[nm])

    o("vector", lambda e: e.memset(ZRO, 0.0), writes=["ZRO"])
    for i in range(5):
        o("vector", lambda e, i=i: e.tensor_scalar(
            NEG[:, :, i], VEC[:, :, i], -1.0, 1.0, ALU.mult, ALU.add),
          reads=[("VEC", i)], writes=[("NEG", i)])
    o("vector", lambda e: e.tensor_scalar(
        NEG[:, :, 5], VEC[:, :, VIX["k_a"]], -1.0, 1.0, ALU.mult, ALU.add),
      reads=[("VEC", VIX["k_a"])], writes=[("NEG", 5)])
    o("vector", lambda e: e.memset(rk2, 0.0), writes=[("rk2", cc) for cc in range(8)])
    for cc in range(8):
        o("scalar", lambda e, cc=cc: e.copy(rk2[0:64, cc, 0:1], rkv[0:64, cc:cc+1]),
          reads=["rkv"], writes=[("rk2", cc)])
        o("scalar", lambda e, cc=cc: e.copy(rk2[64:128, cc, 1:2], rkv[64:128, cc:cc+1]),
          reads=["rkv"], writes=[("rk2", cc)])
    def gwt_t(e):
        last = None
        for cc in range(8):
            last = e.transpose(PSB[0:128, cc*8:(cc+1)*8], gwn[:, cc*128:(cc+1)*128], IDB[0:8, 0:8])
        return last
    o("tensor", gwt_t, reads=["gwn", "IDB", "bPSB"], writes=["bPSB"], tag="gwT")
    o("scalar", lambda e: e.copy(gwT, PSB[:, 0:64].rearrange("p (cc ee) -> p cc ee", ee=8)),
      reads=["bPSB"], writes=["gwT"], tag="gwTe")
    o("vector", lambda e: e.memset(XH_FM[:, :, 0:1], 0.0), writes=["XH_halo"])
    o("vector", lambda e: e.memset(CG, 0.0), writes=[("CG", cc) for cc in range(8)])
    o("vector", lambda e: e.memset(SST, 0.0),
      writes=[("SST", cc, par, bu) for cc in range(8) for par in range(2) for bu in range(2)])
    o("vector", lambda e: e.memset(ZTP, 0.0), writes=[("ZTP", i) for i in range(4)])
    o("vector", lambda e: e.memset(WPD, 0.0), writes=[("WPD", i) for i in range(4)])

    def emit_ln(src_ap, srckeys, out_ap, outkey, tag):
        st = STT
        o("scalar", lambda e: e.activation(DUMP, src_ap, AF.Identity,
                                           accum_out=st[:, 0:1]),
          reads=srckeys + ["DUMP"], writes=["STs", "DUMP"], tag=tag + "sum")
        o("scalar", lambda e: e.activation(DUMP, src_ap, AF.Square,
                                           accum_out=st[:, 1:2]),
          reads=srckeys + ["STs", "DUMP"], writes=["STq", "DUMP"], tag=tag + "ssq")
        o("vector", lambda e: e.tensor_scalar_mul(st[:, 2:3], st[:, 0:1], 1.0 / C),
          reads=["STs"], writes=["STm"], tag=tag + "m")
        o("vector", lambda e: e.tensor_tensor(st[:, 3:4], st[:, 2:3], st[:, 2:3], ALU.mult),
          reads=["STm"], writes=["STm2"], tag=tag + "m2")
        o("vector", lambda e: e.scalar_tensor_tensor(
            st[:, 4:5], st[:, 1:2], 1.0 / C, st[:, 3:4], ALU.mult, ALU.subtract),
          reads=["STq", "STm2"], writes=["STv"], tag=tag + "v")
        o("scalar", lambda e: e.activation(st[:, 5:6], st[:, 4:5], AF.Sqrt, bias=1e-5),
          reads=["STv", "MISC"], writes=["STsd"], tag=tag + "sd")
        o("vector", lambda e: e.reciprocal(st[:, 6:7], st[:, 5:6]),
          reads=["STsd"], writes=["STr"], tag=tag + "r")
        o("vector", lambda e: e.tensor_tensor(st[:, 7:8], st[:, 2:3], st[:, 6:7], ALU.mult),
          reads=["STm", "STr"], writes=["STmr"], tag=tag + "mr")
        o("vector", lambda e: e.tensor_scalar(
            out_ap, src_ap, st[:, 6:7], st[:, 7:8], ALU.mult, ALU.subtract),
          reads=srckeys + ["STr", "STmr", outkey], writes=[outkey], tag=tag + "ap")

    for t in range(NT if BUILD_STAGE >= 1 else 0):
        tb = t % 2
        t0 = t * TT
        gchunk0 = t * 2
        d("sync", lambda e, t0=t0, tb=tb: e.dma_start(
            out=XT[:, tb], in_=x_in[t0:t0 + TT, :]),
          reads=[("XT", tb)], writes=[("XT", tb)], tag=f"xl{t}")

        emit_ln(XT[:, tb], [("XT", tb)], XH_TM, "XHC", f"ln1_{t}")

        if t > 0:
            o("vector", lambda e: e.tensor_copy(XH_FM[:, :, 0:1], XH_FM[:, :, TT:TT+1]),
              reads=[("XH_FM",)], writes=["XH_halo"], tag=f"halo{t}")
        def xh_t(e):
            last = None
            for cc in range(8):
                last = e.transpose(PSB[:, cc*128:(cc+1)*128],
                                   XH_TM[:, cc*128:(cc+1)*128], IDB)
            return last
        o("tensor", xh_t, reads=["XHC", "IDB", "bPSB"], writes=["bPSB"],
          tag=f"xhT{t}")
        o("scalar", lambda e: e.copy(
            XH_FM[:, :, 1:TT+1], PSB.rearrange("p (cc n) -> p cc n", n=TT)),
          reads=["bPSB", "XH_halo", ("XH_FM",)], writes=[("XH_FM",)], tag=f"xhE{t}")

        if ASTAGE < 2:
            continue

        def emit_mix(mi, name):
            mb = mi % 2
            xf = [("XH_FM",), "XH_halo"]
            for cc in range(8):
                o("scalar", lambda e, cc=cc: e.activation(
                    TMPM[:, cc], XH_FM[:, cc, 0:TT], AF.Identity, scale=vcol("x_" + name, cc)),
                  reads=xf + [("VEC", VIX["x_" + name]), ("TMPM", cc)],
                  writes=[("TMPM", cc)], tag=f"ms{name}{cc}")
                o("vector", lambda e, cc=cc, mb=mb: e.scalar_tensor_tensor(
                    MIX[:, mb, cc], XH_FM[:, cc, 1:TT+1],
                    NEG[:, cc, VIX["x_" + name]:VIX["x_" + name]+1],
                    TMPM[:, cc], ALU.mult, ALU.add),
                  reads=xf + [("TMPM", cc), ("NEG", VIX["x_" + name]), ("MIX", mb, cc)],
                  writes=[("MIX", mb, cc)], tag=f"mv{name}{cc}")
            return mb

        def proj_1024(mb, Wsb, wkey, dstbuf, okey, extrakeys=()):
            for oc in range(8):
                ps, pk = (PS0, "bPS0") if oc % 2 == 0 else (PS1, "bPS1")
                def mm(e, oc=oc, mb=mb, Wsb=Wsb, ps=ps):
                    last = None
                    for kc in range(8):
                        last = e.matmul(ps[:, 0:TT], Wsb[:, kc, oc*128:(oc+1)*128],
                                        MIX[:, mb, kc], start=(kc == 0), stop=(kc == 7))
                    return last
                o("tensor", mm, reads=[wkey, pk] + [("MIX", mb, kc) for kc in range(8)],
                  writes=[pk], tag=f"pj{okey}{oc}")
                o("scalar", lambda e, oc=oc, ps=ps, dstbuf=dstbuf: e.copy(
                    dstbuf[:, oc], ps[:, 0:TT]),
                  reads=[pk, (okey, oc)], writes=[(okey, oc)] + list(extrakeys),
                  tag=f"pe{okey}{oc}")

        mb = emit_mix(0, "r")
        proj_1024(mb, Wr, "Wr", R_FM, "R_FM")

        if ASTAGE < 3:
            continue
        mb = emit_mix(1, "w")
        def wl_mm(e, mb=mb):
            last = None
            for kc in range(8):
                last = e.matmul(PS2[0:64, 0:TT], wla[:, kc, :], MIX[:, mb, kc],
                                start=(kc == 0), stop=(kc == 7))
            return last
        o("tensor", wl_mm, reads=["wla", "bPS2"] + [("MIX", mb, kc) for kc in range(8)],
          writes=["bPS2"], tag="wlA")
        if ASTAGE == 31:
            continue
        TNH_ap = SCB[0:64, 0, :]
        o("scalar", lambda e: e.activation(TNH_ap, PS2[0:64, 0:TT], AF.Tanh),
          reads=["bPS2", ("SCR", 0)], writes=[("SCR", 0)], tag="wlT")
        if ASTAGE == 315:
            continue
        for oc in range(8):
            ps, pk = (PS2, "bPS2") if oc % 2 == 0 else (PS3, "bPS3")
            o("tensor", lambda e, oc=oc, ps=ps: e.matmul(
                ps[:, TT:2*TT], wlb[:, oc*128:(oc+1)*128], TNH_ap, start=True, stop=True),
              reads=["wlb", ("SCR", 0), pk], writes=[pk], tag=f"wlB{oc}")
            if ASTAGE == 32:
                o("scalar", lambda e, oc=oc, ps=ps: e.copy(LOGW[:, oc], ps[:, TT:2*TT]),
                  reads=[pk, ("LOGW", oc)], writes=[("LOGW", oc)], tag=f"wCp{oc}")
                continue
            o("scalar", lambda e, oc=oc, ps=ps: e.activation(
                SCR[:, 1, :], ps[:, TT:2*TT], AF.Abs, bias=vcol("w_lora_bias", oc)),
              reads=[pk, ("VEC", VIX["w_lora_bias"]), ("SCR", 1)], writes=[("SCR", 1)],
              tag=f"wA{oc}")
            if ASTAGE == 33:
                o("scalar", lambda e, oc=oc: e.copy(LOGW[:, oc], SCR[:, 1, :]),
                  reads=[("SCR", 1), ("LOGW", oc)], writes=[("LOGW", oc)], tag=f"wCp{oc}")
                continue
            o("scalar", lambda e: e.activation(SCR[:, 2, :], SCR[:, 1, :], AF.Exp, scale=-1.0),
              reads=[("SCR", 1), ("SCR", 2)], writes=[("SCR", 2)], tag=f"wE{oc}")
            if ASTAGE == 34:
                o("scalar", lambda e, oc=oc: e.copy(LOGW[:, oc], SCR[:, 2, :]),
                  reads=[("SCR", 2), ("LOGW", oc)], writes=[("LOGW", oc)], tag=f"wCp{oc}")
                continue
            o("scalar", lambda e: e.activation(SCR[:, 3, :], SCR[:, 2, :], AF.Ln, bias=1.0),
              reads=[("SCR", 2), ("SCR", 3), "MISC"], writes=[("SCR", 3)], tag=f"wL{oc}")
            if ASTAGE == 345:
                o("scalar", lambda e, oc=oc: e.copy(LOGW[:, oc], SCR[:, 3, :]),
                  reads=[("SCR", 3), ("LOGW", oc)], writes=[("LOGW", oc)], tag=f"wCp{oc}")
                continue
            o("vector", lambda e, oc=oc, ps=ps: e.tensor_scalar(
                SCR[:, 4, :], ps[:, TT:2*TT], vcol("w_lora_bias", oc), 0.0, ALU.add, ALU.min),
              reads=[pk, ("VEC", VIX["w_lora_bias"]), ("SCR", 4), ("SCR", 1)],
              writes=[("SCR", 4)], tag=f"wM{oc}")
            if ASTAGE == 35:
                o("vector", lambda e, oc=oc: e.tensor_copy(LOGW[:, oc], SCR[:, 4, :]),
                  reads=[("SCR", 4), ("LOGW", oc)], writes=[("LOGW", oc)], tag=f"wCp{oc}")
                continue
            o("vector", lambda e, oc=oc: e.scalar_tensor_tensor(
                LOGW[:, oc], SCR[:, 4, :], -0.5, SCR[:, 3, :], ALU.add, ALU.subtract),
              reads=[("SCR", 4), ("SCR", 3), ("LOGW", oc)], writes=[("LOGW", oc)],
              tag=f"wF{oc}")

        if ASTAGE < 4:
            continue
        mb = emit_mix(2, "k")
        proj_1024(mb, Wk, "Wk", K_FM, "K_FM", extrakeys=["KPRK"])
        for cc in range(8):
            o("scalar", lambda e, cc=cc: e.activation(
                KKR[:, cc], K_FM[:, cc], AF.Identity, scale=vcol("k_k", cc)),
              reads=[("K_FM", cc), ("VEC", VIX["k_k"]), ("KKR", cc), "KPRK"],
              writes=[("KKR", cc), "KKRG"], tag=f"kr{cc}")
        mb = emit_mix(3, "v")
        proj_1024(mb, Wv, "Wv", V_FM, "V_FM")
        mb = emit_mix(4, "a")
        def al_mm(e, mb=mb):
            last = None
            for kc in range(8):
                last = e.matmul(PS2[0:64, 0:TT], ala[:, kc, :], MIX[:, mb, kc],
                                start=(kc == 0), stop=(kc == 7))
            return last
        o("tensor", al_mm, reads=["ala", "bPS2"] + [("MIX", mb, kc) for kc in range(8)],
          writes=["bPS2"], tag="alA")
        ZA1_ap = SCB[0:64, 5, :]
        o("scalar", lambda e: e.copy(ZA1_ap, PS2[0:64, 0:TT]),
          reads=["bPS2", ("SCR", 5)], writes=[("SCR", 5)], tag="alE")
        for oc in range(8):
            ps, pk = (PS2, "bPS2") if oc % 2 == 0 else (PS3, "bPS3")
            o("tensor", lambda e, oc=oc, ps=ps: e.matmul(
                ps[:, TT:2*TT], alb[:, oc*128:(oc+1)*128], ZA1_ap, start=True, stop=True),
              reads=["alb", ("SCR", 5), pk], writes=[pk], tag=f"alB{oc}")
            o("scalar", lambda e, oc=oc, ps=ps: e.activation(
                A_FM[:, oc], ps[:, TT:2*TT], AF.Sigmoid, bias=vcol("a_lora_bias", oc)),
              reads=[pk, ("VEC", VIX["a_lora_bias"]), ("A_FM", oc)],
              writes=[("A_FM", oc)], tag=f"aS{oc}")

        if ASTAGE < 5:
            continue
        KSQ_ap = SCB[:, 6, :]
        KIV_ap = SCB[0:2, 2, :]
        for cc in range(8):
            o("scalar", lambda e, cc=cc: e.activation(KSQ_ap, KKR[:, cc], AF.Square),
              reads=[("KKR", cc), ("SCR", 6), "KKRG"], writes=[("SCR", 6)], tag=f"kq{cc}")
            o("tensor", lambda e: e.matmul(PS6[0:2, 0:TT], ON2, KSQ_ap,
                                           start=True, stop=True),
              reads=[("SCR", 6), "ON2", "bPS6"], writes=["bPS6"], tag=f"kqm{cc}")
            o("scalar", lambda e: e.activation(SC2[0:2, 0, 0:TT], PS6[0:2, 0:TT], AF.Sqrt),
              reads=["bPS6", ("SC2", 0)], writes=[("SC2", 0)], tag=f"kqs{cc}")
            o("vector", lambda e: e.tensor_scalar_max(SC2[0:2, 1, 0:TT], SC2[0:2, 0, 0:TT], 1e-12),
              reads=[("SC2", 0), ("SC2", 1)], writes=[("SC2", 1)], tag=f"kqx{cc}")
            o("vector", lambda e: e.reciprocal(KIV_ap, SC2[0:2, 1, 0:TT]),
              reads=[("SC2", 1), ("SC2", 2)], writes=[("SC2", 2)], tag=f"kqi{cc}")
            o("tensor", lambda e: e.matmul(PS6[:, TT:2*TT], EX2, KIV_ap,
                                           start=True, stop=True),
              reads=[("SC2", 2), "EX2", "bPS6"], writes=["bPS6"], tag=f"kqb{cc}")
            o("vector", lambda e, cc=cc: e.tensor_tensor(
                KK_FM[:, cc], KKR[:, cc], PS6[:, TT:2*TT], ALU.mult),
              reads=["bPS6", ("KKR", cc), ("KK_FM", cc), "KKRG"],
              writes=[("KK_FM", cc)], tag=f"kqf{cc}")

        def gl_mm(e):
            last = None
            for kc in range(8):
                last = e.matmul(PS4[:, 0:TT], gla[:, kc, :], R_FM[:, kc],
                                start=(kc == 0), stop=(kc == 7))
            return last
        o("tensor", gl_mm, reads=["gla", "bPS4"] + [("R_FM", kc) for kc in range(8)],
          writes=["bPS4"], tag="glA")
        SGZ_ap = SCB[:, 7, :]
        o("scalar", lambda e: e.activation(SGZ_ap, PS4[:, 0:TT], AF.Sigmoid),
          reads=["bPS4", ("SCR", 7)], writes=[("SCR", 7)], tag="glS")
        for oc in range(8):
            ps, pk = (PS4, "bPS4") if oc % 2 == 0 else (PS5, "bPS5")
            o("tensor", lambda e, oc=oc, ps=ps: e.matmul(
                ps[:, TT:2*TT], glb[:, oc*128:(oc+1)*128], SGZ_ap, start=True, stop=True),
              reads=["glb", ("SCR", 7), pk], writes=[pk], tag=f"glB{oc}")
            o("scalar", lambda e, oc=oc, ps=ps: e.activation(
                G_FM[:, oc], ps[:, TT:2*TT], AF.Identity, bias=vcol("g_lora_bias", oc)),
              reads=[pk, ("VEC", VIX["g_lora_bias"]), ("KKR", oc)],
              writes=[("KKR", oc), ("G", oc), "KKRG"], tag=f"gF{oc}")

        K2S_ap = SCB[:, 3, :]
        for cc in range(8):
            o("vector", lambda e, cc=cc: e.tensor_scalar(
                K2S_ap, A_FM[:, cc], vcol("k_a", cc), NEG[:, cc, 5:6], ALU.mult, ALU.add),
              reads=[("A_FM", cc), ("VEC", VIX["k_a"]), ("NEG", 5), ("SCR", 6)],
              writes=[("SCR", 6)], tag=f"k2a{cc}")
            o("vector", lambda e, cc=cc: e.tensor_tensor(
                K2_FM[:, cc], K_FM[:, cc], K2S_ap, ALU.mult),
              reads=[("SCR", 6), ("K_FM", cc), ("K2_FM", cc), "KPRK"],
              writes=[("K2_FM", cc)], tag=f"k2b{cc}")

        if ASTAGE < 6:
            continue
        for cc in range(8):
            o("vector", lambda e, cc=cc: e.tensor_copy(CG[:, cc, 0:1], CG[:, cc, TT:TT+1]),
              reads=[("CG", cc)], writes=[("CG", cc)], tag=f"cc{cc}")
            o("vector", lambda e, cc=cc: e.tensor_tensor_scan(
                CG[:, cc, 1:TT+1], LOGW[:, cc], ON1.broadcast_to([128, TT]),
                CG[:, cc, 0:1], ALU.add, ALU.mult),
              reads=[("LOGW", cc), "ON1", ("CG", cc)], writes=[("CG", cc)],
              tag=f"cs{cc}")
            o("vector", lambda e, cc=cc: e.tensor_copy(CB[:, cc, :], CG[:, cc, 0:TT+1:LCH]),
              reads=[("CG", cc), ("CB", cc)], writes=[("CB", cc)], tag=f"cb{cc}")
            o("vector", lambda e, cc=cc: e.tensor_add(
                PMID[:, cc, :], CB[:, cc, 0:2], CB[:, cc, 1:3]),
              reads=[("CB", cc), ("PMID", cc)], writes=[("PMID", cc)], tag=f"p1{cc}")
            o("vector", lambda e, cc=cc: e.tensor_scalar_mul(PMID[:, cc, :], PMID[:, cc, :], 0.5),
              reads=[("PMID", cc)], writes=[("PMID", cc)], tag=f"p2{cc}")
            o("vector", lambda e, cc=cc: e.tensor_sub(ECL[:, cc, :], CB[:, cc, 1:3], CB[:, cc, 0:2]),
              reads=[("CB", cc), ("ECL", cc)], writes=[("ECL", cc)], tag=f"dc{cc}")
            o("scalar", lambda e, cc=cc: e.activation(EPHI[:, cc, :], ECL[:, cc, :], AF.Exp, scale=0.5),
              reads=[("ECL", cc), ("EPHI", cc)], writes=[("EPHI", cc)], tag=f"ep{cc}")
            o("scalar", lambda e, cc=cc: e.activation(ECL[:, cc, :], ECL[:, cc, :], AF.Exp),
              reads=[("ECL", cc), ("EPHI", cc)], writes=[("ECL", cc)], tag=f"ec{cc}")

        for cc in range(8):
            cg1 = CG[:, cc, 1:TT+1]
            cg0 = CG[:, cc, 0:TT]
            pmb = PMID[:, cc, :].unsqueeze(2).broadcast_to([128, 2, LCH])
            c0b = CB[:, cc, 0:2].unsqueeze(2).broadcast_to([128, 2, LCH])
            v3 = lambda ap: ap.rearrange("p (ch l) -> p ch l", l=LCH)
            ESC = SCR[:, 1, :]
            def esub(src3, bca, rkeys, tag_):
                o("vector", lambda e, src3=src3, bca=bca: e.tensor_tensor(
                    v3(ESC), src3, bca, ALU.subtract),
                  reads=rkeys + [("SCR", 1)], writes=[("SCR", 1)], tag=tag_)
            esub(v3(cg1), pmb, [("CG", cc), ("PMID", cc)], f"e1{cc}")
            o("scalar", lambda e, cc=cc: e.activation(FKB[:, cc], ESC, AF.Exp, scale=-1.0),
              reads=[("SCR", 1), ("FKB", cc)], writes=[("FKB", cc)], tag=f"fk{cc}")
            o("scalar", lambda e, cc=cc: e.activation(FRT[:, cc], ESC, AF.Exp),
              reads=[("SCR", 1), ("FRT", cc)], writes=[("FRT", cc)], tag=f"fr{cc}")
            o("vector", lambda e, cc=cc: e.tensor_tensor(
                SAR[:, cc, 1], R_FM[:, cc], FRT[:, cc], ALU.mult),
              reads=[("R_FM", cc), ("FRT", cc), ("SAR", cc, 1)], writes=[("SAR", cc, 1)],
              tag=f"pr{cc}")
            o("gpsimd", lambda e, cc=cc: e.tensor_tensor(
                SB_[:, cc], KK_FM[:, cc], A_FM[:, cc], ALU.mult),
              reads=[("KK_FM", cc), ("A_FM", cc), ("SB_", cc)], writes=[("SB_", cc)],
              tag=f"pb1{cc}")
            o("gpsimd", lambda e, cc=cc: e.tensor_tensor(
                SB_[:, cc], SB_[:, cc], FKB[:, cc], ALU.mult),
              reads=[("FKB", cc), ("SB_", cc)], writes=[("SB_", cc)], tag=f"pb2{cc}")
            o("gpsimd", lambda e, cc=cc: e.tensor_tensor(
                SK_[:, cc], K2_FM[:, cc], FKB[:, cc], ALU.mult),
              reads=[("K2_FM", cc), ("FKB", cc), ("SK_", cc)], writes=[("SK_", cc)],
              tag=f"pk{cc}")
            esub(v3(cg0), pmb, [("CG", cc), ("PMID", cc)], f"e3{cc}")
            o("scalar", lambda e, cc=cc: e.activation(FSH[:, cc], ESC, AF.Exp),
              reads=[("SCR", 1), ("FSH", cc)], writes=[("FSH", cc)], tag=f"fa{cc}")
            o("vector", lambda e, cc=cc: e.scalar_tensor_tensor(
                SAR[:, cc, 0], KK_FM[:, cc], -1.0, FSH[:, cc], ALU.mult, ALU.mult),
              reads=[("KK_FM", cc), ("FSH", cc), ("SAR", cc, 0)], writes=[("SAR", cc, 0)],
              tag=f"pa{cc}")
            esub(v3(cg0), c0b, [("CG", cc), ("CB", cc), ("SAR", cc, 0)], f"e4{cc}")
            o("scalar", lambda e, cc=cc: e.activation(FSH[:, cc], ESC, AF.Exp),
              reads=[("SCR", 1), ("FSH", cc), ("SAR", cc, 0)], writes=[("FSH", cc)],
              tag=f"fh{cc}")
            o("vector", lambda e, cc=cc: e.scalar_tensor_tensor(
                SAH[:, cc], KK_FM[:, cc], -1.0, FSH[:, cc], ALU.mult, ALU.mult),
              reads=[("KK_FM", cc), ("FSH", cc), ("SAH", cc)], writes=[("SAH", cc)],
              tag=f"ph{cc}")
            esub(v3(cg1), c0b, [("CG", cc), ("CB", cc), ("SAH", cc)], f"e5{cc}")
            o("scalar", lambda e, cc=cc: e.activation(FSH[:, cc], ESC, AF.Exp),
              reads=[("SCR", 1), ("FSH", cc), ("SAH", cc)], writes=[("FSH", cc)],
              tag=f"fh2{cc}")
            o("gpsimd", lambda e, cc=cc: e.tensor_tensor(
                SRH[:, cc], R_FM[:, cc], FSH[:, cc], ALU.mult),
              reads=[("R_FM", cc), ("FSH", cc), ("SRH", cc)], writes=[("SRH", cc)],
              tag=f"prh{cc}")

        if DBG:
            def tapfm(name, buf, keys):
                ap = dbg[name][:, t0:t0+TT].rearrange("(kc p) tt -> p kc tt", p=128)
                d("gpsimd", lambda e, ap=ap, buf=buf: e.dma_start(out=ap, in_=buf),
                  reads=keys, tag=f"tp{name}")
            tapfm("d_hs", XH_FM[:, :, 1:TT+1], [("XH_FM",)])
            tapfm("d_r", R_FM, [("R_FM", i) for i in range(8)])
            tapfm("d_logw", LOGW, [("LOGW", i) for i in range(8)])
            tapfm("d_k2", K2_FM, [("K2_FM", i) for i in range(8)])
            tapfm("d_a", A_FM, [("A_FM", i) for i in range(8)])
            tapfm("d_kk", KK_FM, [("KK_FM", i) for i in range(8)])
            tapfm("d_g", G_FM, [("G", i) for i in range(8)] + ["KKRG"])
            tapfm("d_v", V_FM, [("V_FM", i) for i in range(8)])

        if BUILD_STAGE < 2:
            continue

        for h in range(NH):
            cc = h // 2
            bh = (h % 2) * 64
            rot = h % 2
            zrot = (h % 2) * 2 + (t % 2)
            I = (t, h)
            a_sl = SAR[bh:bh+64, cc, 0, :]
            r_sl = SAR[bh:bh+64, cc, 1, :]
            ar_sl = SAR[bh:bh+64, cc, 0:2, :]
            b_sl = SB_[bh:bh+64, cc, :]
            k_sl = SK_[bh:bh+64, cc, :]
            v_sl = V_FM[bh:bh+64, cc, :]
            ah_sl = SAH[bh:bh+64, cc, :]
            rh_sl = SRH[bh:bh+64, cc, :]
            pkeys = [("SAR", cc, 0), ("SAR", cc, 1), ("SB_", cc), ("SK_", cc),
                     ("V_FM", cc), ("SAH", cc), ("SRH", cc)]

            def tps(e, ah_sl=ah_sl, b_sl=b_sl, k_sl=k_sl, v_sl=v_sl, bh=bh):
                idn = IDB[bh:bh+64, bh:bh+64]
                e.transpose(PSB[:, 0:64], b_sl, idn)
                e.transpose(PSB[:, 64:128], k_sl, idn)
                e.transpose(PSB[:, 128:192], v_sl, idn)
                return e.transpose(PSB[:, 192:256], ah_sl, idn)
            o("tensor", tps, reads=pkeys + ["IDB", "bPSB"], writes=["bPSB"],
              tag=f"sT{I}")
            o("scalar", lambda e, rot=rot: e.copy(TMX[:, rot, :], PSB[:, 0:192]),
              reads=["bPSB", ("TMX", rot)], writes=[("TMX", rot)], tag=f"sTe{I}")
            o("scalar", lambda e, rot=rot: e.copy(YST[:, rot, 0, 0:64], PSB[:, 192:256]),
              reads=["bPSB", ("YST", rot, 0)], writes=[("YST", rot, 0)], tag=f"sTf{I}")

            def grams(e, a_sl=a_sl, b_sl=b_sl, k_sl=k_sl, ar_sl=ar_sl):
                e.matmul(PS0[:, 0:128], a_sl, b_sl, start=True, stop=True)
                e.matmul(PS0[:, 128:384], b_sl, ar_sl, start=True, stop=True)
                return e.matmul(PS1[:, 0:256], k_sl, ar_sl, start=True, stop=True)
            o("tensor", grams, reads=pkeys + ["bPS0", "bPS1"],
              writes=["bPS0", "bPS1"], tag=f"sG{I}")
            o("vector", lambda e, rot=rot: e.select(MPL[:, rot, 0], maskA, PS0[:, 0:128], Z128, add_drain=True),
              reads=["bPS0", "MSK", "ZRO", ("MPL", rot, 0)], writes=[("MPL", rot, 0)],
              tag=f"mA{I}")
            o("vector", lambda e, rot=rot: e.select(MTR[:, rot, 0], maskB, PS0[:, 128:256], Z128, add_drain=True),
              reads=["bPS0", "MSK", "ZRO", ("MTR", rot, 0)], writes=[("MTR", rot, 0)],
              tag=f"mB{I}")
            o("vector", lambda e, rot=rot: e.select(BRU[:, rot, :], maskC, PS0[:, 256:384], Z128, add_drain=True),
              reads=["bPS0", "MSK", "ZRO", ("BRU", rot)], writes=[("BRU", rot)],
              tag=f"mR{I}")
            o("vector", lambda e, rot=rot: e.select(CTM[:, rot, :], maskB, PS1[:, 0:128], Z128, add_drain=True),
              reads=["bPS1", "MSK", "ZRO", ("CTM", rot)], writes=[("CTM", rot)],
              tag=f"mC{I}")
            o("vector", lambda e, rot=rot: e.select(KRU[:, rot, :], maskC, PS1[:, 128:256], Z128, add_drain=True),
              reads=["bPS1", "MSK", "ZRO", ("KRU", rot)], writes=[("KRU", rot)],
              tag=f"mK{I}")

            o("tensor", lambda e, rot=rot: e.matmul(
                PS2[:, 0:64], CTM[:, rot, :], TMX[:, rot, 128:192], start=True, stop=True),
              reads=[("CTM", rot), ("TMX", rot), "bPS2"], writes=["bPS2"], tag=f"sC{I}")
            o("scalar", lambda e, rot=rot: e.copy(YST[:, rot, 0, 64:128], PS2[:, 0:64]),
              reads=["bPS2", ("YST", rot, 0)], writes=[("YST", rot, 0)], tag=f"sCe{I}")

            ycur = 0
            mcur = 0
            for lvl in range(6):
                o("tensor", lambda e, rot=rot, mcur=mcur, ycur=ycur: e.matmul(
                    PS2[:, 64:192], MTR[:, rot, mcur], YST[:, rot, ycur, :],
                    start=True, stop=True),
                  reads=[("MTR", rot, mcur), ("YST", rot, ycur), "bPS2"],
                  writes=["bPS2"], tag=f"ap{lvl}{I}")
                o("vector", lambda e, rot=rot, ycur=ycur: e.tensor_add(
                    YST[:, rot, 1 - ycur, :], PS2[:, 64:192], YST[:, rot, ycur, :]),
                  reads=["bPS2", ("YST", rot, ycur), ("YST", rot, 1 - ycur)],
                  writes=[("YST", rot, 1 - ycur)], tag=f"ae{lvl}{I}")
                ycur = 1 - ycur
                if lvl < 5:
                    def sq(e, rot=rot, mcur=mcur):
                        e.matmul(PS3[:, 0:128], MPL[:, rot, mcur], MTR[:, rot, mcur],
                                 start=True, stop=True)
                        return e.matmul(PS3[:, 128:256], MTR[:, rot, mcur],
                                        MPL[:, rot, mcur], start=True, stop=True)
                    o("tensor", sq, reads=[("MPL", rot, mcur), ("MTR", rot, mcur), "bPS3"],
                      writes=["bPS3"], tag=f"sq{lvl}{I}")
                    o("scalar", lambda e, rot=rot, mcur=mcur: e.copy(
                        MTR[:, rot, 1 - mcur], PS3[:, 0:128]),
                      reads=["bPS3", ("MTR", rot, 1 - mcur)],
                      writes=[("MTR", rot, 1 - mcur)], tag=f"sa{lvl}{I}")
                    o("scalar", lambda e, rot=rot, mcur=mcur: e.copy(
                        MPL[:, rot, 1 - mcur], PS3[:, 128:256]),
                      reads=["bPS3", ("MPL", rot, 1 - mcur)],
                      writes=[("MPL", rot, 1 - mcur)], tag=f"sb{lvl}{I}")
                    mcur = 1 - mcur
            ykey = ("YST", rot, ycur)

            o("tensor", lambda e, rot=rot, ycur=ycur: e.matmul(
                PS3[0:64, 256:384], YST[:, rot, ycur, 0:64], BRU[:, rot, :],
                start=True, stop=True),
              reads=[ykey, ("BRU", rot), "bPS3"], writes=["bPS3"], tag=f"zt{I}")
            o("vector", lambda e, zrot=zrot, rh_sl=rh_sl, bh=bh: e.tensor_add(
                ZTP[bh:bh+64, zrot, :], PS3[0:64, 256:384], rh_sl),
              reads=["bPS3", ("SRH", cc), ("ZTP", zrot)], writes=[("ZTP", zrot)],
              tag=f"ze{I}")

            for ch in range(2):
                cb0 = ch * 64
                chk = (t, h, ch)
                o("tensor", lambda e, rot=rot, cb0=cb0, ycur=ycur: e.matmul(
                    PS4[0:64, 0:64], YST[cb0:cb0+64, rot, ycur, 0:64],
                    TMX[cb0:cb0+64, rot, 0:64], start=True, stop=True),
                  reads=[ykey, ("TMX", rot), "bPS4"], writes=["bPS4"], tag=f"w{chk}")
                o("scalar", lambda e, zrot=zrot, bh=bh: e.copy(
                    WPD[bh:bh+64, zrot, :], PS4[0:64, 0:64]),
                  reads=["bPS4", ("WPD", zrot)], writes=[("WPD", zrot)], tag=f"we{chk}")
                def yloc(e, rot=rot, cb0=cb0, ycur=ycur):
                    e.matmul(PS4[0:64, 64:128], TMX[cb0:cb0+64, rot, 0:64],
                             YST[cb0:cb0+64, rot, ycur, 64:128], start=True, stop=False)
                    return e.matmul(PS4[0:64, 64:128], TMX[cb0:cb0+64, rot, 64:128],
                                    TMX[cb0:cb0+64, rot, 128:192], start=False, stop=True)
                o("tensor", yloc, reads=[ykey, ("TMX", rot), "bPS4"],
                  writes=["bPS4"], tag=f"y{chk}")
                ephi_col = EPHI[bh:bh+64, cc, ch:ch+1]
                o("vector", lambda e, ch=ch, ephi_col=ephi_col, bh=bh: e.tensor_tensor(
                    YLC[bh:bh+64, ch, :], PS4[0:64, 64:128],
                    ephi_col.broadcast_to([64, 64]), ALU.mult),
                  reads=["bPS4", ("EPHI", cc), ("YLC", rot, ch)],
                  writes=[("YLC", rot, ch)], tag=f"ye{chk}")

            for ch in range(2):
                gch = gchunk0 + ch
                sbuf = gch % 2
                chk = (t, h, ch)
                statekey = ("SST", cc, h % 2, sbuf)
                nstatekey = ("SST", cc, h % 2, 1 - sbuf)
                ecl_col = ECL[bh:bh+64, cc, ch:ch+1]
                ephi_col = EPHI[bh:bh+64, cc, ch:ch+1]

                def o_mms(e, rot=rot, zrot=zrot, ch=ch, cc=cc, sbuf=sbuf, ycur=ycur):
                    e.matmul(PS5[0:64, ch*64:(ch+1)*64], BRU[:, rot, ch*64:(ch+1)*64],
                             YST[:, rot, ycur, 64:128], start=True, stop=False)
                    e.matmul(PS5[0:64, ch*64:(ch+1)*64], KRU[:, rot, ch*64:(ch+1)*64],
                             TMX[:, rot, 128:192], start=False, stop=False)
                    return e.matmul(PS5[0:64, ch*64:(ch+1)*64], ZTP[:, zrot, ch*64:(ch+1)*64],
                                    SST[:, cc, sbuf, :], start=False, stop=True)
                o("tensor", o_mms,
                  reads=[ykey, ("BRU", rot), ("KRU", rot), ("TMX", rot), ("ZTP", zrot),
                         statekey, "bPS5"],
                  writes=["bPS5"], tag=f"om{chk}")
                o("scalar", lambda e, h=h, ch=ch: e.copy(
                    OTM[ch*64:(ch+1)*64, h*64:(h+1)*64], PS5[0:64, ch*64:(ch+1)*64]),
                  reads=["bPS5", ("OTM", h), "OX2"], writes=[("OTM", h), "OX2"],
                  tag=f"oe{chk}")

                def st_mm(e, zrot=zrot, cc=cc, sbuf=sbuf, bh=bh):
                    e.matmul(PS6[0:64, 448:512], WPD[:, zrot, :], SST[:, cc, sbuf, :],
                             start=True, stop=False)
                    return e.matmul(PS6[0:64, 448:512], IDB[:, bh:bh+64],
                                    SST[:, cc, sbuf, :], start=False, stop=True)
                o("tensor", st_mm, reads=[("WPD", zrot), statekey, "IDB", "bPS6"],
                  writes=["bPS6"], tag=f"st{chk}")
                o("vector", lambda e, cc=cc, sbuf=sbuf, bh=bh, ecl_col=ecl_col: e.tensor_tensor(
                    STMP[bh:bh+64, :], SST[bh:bh+64, cc, sbuf, :],
                    ecl_col.broadcast_to([64, 64]), ALU.mult),
                  reads=[statekey, ("ECL", cc), ("STMP", rot)],
                  writes=[("STMP", rot)], tag=f"s1{chk}")
                o("vector", lambda e, bh=bh, ch=ch: e.tensor_add(
                    STMP[bh:bh+64, :], STMP[bh:bh+64, :], YLC[bh:bh+64, ch, :]),
                  reads=[("YLC", rot, ch), ("STMP", rot)],
                  writes=[("STMP", rot)], tag=f"s1b{chk}")
                o("vector", lambda e, cc=cc, sbuf=sbuf, bh=bh, ephi_col=ephi_col: e.tensor_tensor(
                    SST[bh:bh+64, cc, 1 - sbuf, :], PS6[0:64, 448:512],
                    ephi_col.broadcast_to([64, 64]), ALU.mult),
                  reads=["bPS6", ("EPHI", cc), nstatekey],
                  writes=[nstatekey], tag=f"s2{chk}")
                o("vector", lambda e, cc=cc, sbuf=sbuf, bh=bh: e.tensor_add(
                    SST[bh:bh+64, cc, 1 - sbuf, :], SST[bh:bh+64, cc, 1 - sbuf, :],
                    STMP[bh:bh+64, :]),
                  reads=[("STMP", rot), nstatekey],
                  writes=[nstatekey], tag=f"s2b{chk}")

        if DBG:
            ap = dbg["d_o"][t0:t0+TT, :]
            d("gpsimd", lambda e, ap=ap: e.dma_start(out=ap, in_=OTM),
              reads=[("OTM", h) for h in range(NH)] + ["OX2"], tag="tpo")

        if BUILD_STAGE < 3:
            continue

        otkeys = [("OTM", h) for h in range(NH)] + ["OX2"]
        o3v = OTM.rearrange("p (s dd) -> p s dd", dd=HD)
        o("vector", lambda e, o3v=o3v: e.reduce_sum(MU, o3v, AX.X),
          reads=otkeys + ["MUb"], writes=["MUb"], tag="gm")
        o("scalar", lambda e: e.activation(DUMP, OTM, AF.Square),
          reads=otkeys + ["DUMP"], writes=["DUMP"], tag="gq")
        o("vector", lambda e: e.reduce_sum(
            SSQ, DUMP.rearrange("p (s dd) -> p s dd", dd=HD), AX.X),
          reads=["DUMP", "SSQb"], writes=["SSQb"], tag="gq2")
        o("vector", lambda e: e.tensor_scalar_mul(MU, MU, 1.0 / HD),
          reads=["MUb"], writes=["MUb"], tag="gm2")
        o("vector", lambda e: e.tensor_tensor(MRS, MU, MU, ALU.mult),
          reads=["MUb", "MRSb"], writes=["MRSb"], tag="gm3")
        o("vector", lambda e: e.scalar_tensor_tensor(
            SSQ, SSQ, 1.0 / HD, MRS, ALU.mult, ALU.subtract),
          reads=["SSQb", "MRSb"], writes=["SSQb"], tag="gv")
        o("scalar", lambda e: e.activation(SSQ, SSQ, AF.Sqrt, bias=1e-5),
          reads=["SSQb", "MISC"], writes=["SSQb"], tag="gs")
        o("vector", lambda e: e.reciprocal(RSTD, SSQ),
          reads=["SSQb", "RSTDb"], writes=["RSTDb"], tag="gr")
        o("vector", lambda e: e.tensor_tensor(MRS, MU, RSTD, ALU.mult),
          reads=["MUb", "RSTDb", "MRSb"], writes=["MRSb"], tag="gmr")
        o("vector", lambda e, o3v=o3v: e.tensor_tensor(
            ON_TM.rearrange("p (s dd) -> p s dd", dd=HD), o3v,
            RSTD.unsqueeze(2).broadcast_to([128, NH, HD]), ALU.mult),
          reads=otkeys + ["RSTDb", "XHC"], writes=["XHC"], tag="ga")
        o("vector", lambda e: e.tensor_tensor(
            ON_TM.rearrange("p (s dd) -> p s dd", dd=HD),
            ON_TM.rearrange("p (s dd) -> p s dd", dd=HD),
            MRS.unsqueeze(2).broadcast_to([128, NH, HD]), ALU.subtract),
          reads=["XHC", "MRSb"], writes=["XHC"], tag="gb")
        def on_t(e):
            last = None
            for cc in range(8):
                last = e.transpose(PSB[:, cc*128:(cc+1)*128],
                                   ON_TM[:, cc*128:(cc+1)*128], IDB)
            return last
        o("tensor", on_t, reads=["XHC", "IDB", "bPSB"], writes=["bPSB"], tag="oT")
        o("scalar", lambda e: e.copy(ON_FM, PSB.rearrange("p (cc n) -> p cc n", n=TT)),
          reads=["bPSB", "ON_FMb"], writes=["ON_FMb"], tag="oTe")
        for cc in range(8):
            o("vector", lambda e, cc=cc: e.tensor_tensor(
                PRK[:, cc], R_FM[:, cc], K2_FM[:, cc], ALU.mult),
              reads=[("R_FM", cc), ("K2_FM", cc), ("K_FM", cc), "KPRK"],
              writes=[("K_FM", cc), "KPRK"], tag=f"bp{cc}")
            o("tensor", lambda e, cc=cc: e.matmul(
                PS0[0:2, 0:TT], rk2[:, cc, :], PRK[:, cc], start=True, stop=True),
              reads=[("K_FM", cc), ("rk2", cc), "bPS0"], writes=["bPS0"], tag=f"bs{cc}")
            o("scalar", lambda e, cc=cc: e.copy(SBC[:, cc, :], PS0[0:2, 0:TT]),
              reads=["bPS0", ("SBC", cc)], writes=[("SBC", cc)], tag=f"be{cc}")
            o("tensor", lambda e, cc=cc: e.matmul(
                PS1[:, 0:TT], EX2, SBC[:, cc, :], start=True, stop=True),
              reads=[("SBC", cc), "EX2", "bPS1"], writes=["bPS1"], tag=f"bb{cc}")
            o("vector", lambda e, cc=cc: e.tensor_tensor(
                OG[:, cc], PS1[:, 0:TT], V_FM[:, cc], ALU.mult),
              reads=["bPS1", ("V_FM", cc), ("OG", cc)], writes=[("OG", cc)], tag=f"bv{cc}")
            o("vector", lambda e, cc=cc: e.tensor_add(OG[:, cc], OG[:, cc], ON_FM[:, cc, :]),
              reads=[("OG", cc), "ON_FMb"], writes=[("OG", cc)], tag=f"bo{cc}")
            o("vector", lambda e, cc=cc: e.tensor_scalar(
                OG[:, cc], OG[:, cc], vcol("gn_weight", cc), vcol("gn_bias", cc),
                ALU.mult, ALU.add),
              reads=[("OG", cc), ("VEC", VIX["gn_weight"]), ("VEC", VIX["gn_bias"])],
              writes=[("OG", cc)], tag=f"bw{cc}")
            o("vector", lambda e, cc=cc: e.tensor_tensor(OG[:, cc], OG[:, cc], G_FM[:, cc], ALU.mult),
              reads=[("OG", cc), ("G", cc), "KKRG"], writes=[("OG", cc)], tag=f"bg{cc}")

        if DBG:
            ap = dbg["d_of"][:, t0:t0+TT].rearrange("(kc p) tt -> p kc tt", p=128)
            d("gpsimd", lambda e, ap=ap: e.dma_start(out=ap, in_=OG),
              reads=[("OG", cc) for cc in range(8)], tag="tpof")

        def womm(e):
            last = None
            for kc in range(8):
                last = e.matmul(PS0[:, 0:512], OG[:, kc, :], Wo[:, kc, 0:512],
                                start=(kc == 0), stop=(kc == 7))
            for kc in range(8):
                last = e.matmul(PS1[:, 0:512], OG[:, kc, :], Wo[:, kc, 512:1024],
                                start=(kc == 0), stop=(kc == 7))
            return last
        o("tensor", womm,
          reads=[("OG", cc) for cc in range(8)] + ["Wo", "bPS0", "bPS1"],
          writes=["bPS0", "bPS1"], tag="wo")
        o("vector", lambda e, tb=tb: e.tensor_add(
            X2T[:, 0:512], PS0[:, 0:512], XT[:, tb, 0:512]),
          reads=["bPS0", ("XT", tb)], writes=["X2Ta", "OX2"], tag="x2a")
        o("vector", lambda e, tb=tb: e.tensor_add(
            X2T[:, 512:1024], PS1[:, 0:512], XT[:, tb, 512:1024]),
          reads=["bPS1", ("XT", tb)], writes=["X2Tb", "OX2"], tag="x2b")
        d("sync", lambda e, t0=t0: e.dma_start(out=x2s[t0:t0+TT, :], in_=X2T),
          reads=["X2Ta", "X2Tb", "OX2"], writes=[("x2sw", t)], tag=f"x2w{t}")
        kb.last_w[("x2s", t // 2)] = kb.last_w[("x2sw", t)]
        emit_ln(X2T, ["X2Ta", "X2Tb", "OX2"], Y_TM, "XHC", f"ln2_{t}")
        def y_t(e):
            last = None
            for cc in range(8):
                last = e.transpose(PSB[:, cc*128:(cc+1)*128],
                                   Y_TM[:, cc*128:(cc+1)*128], IDB)
            return last
        o("tensor", y_t, reads=["XHC", "IDB", "bPSB"], writes=["bPSB"], tag="yT")
        o("scalar", lambda e: e.copy(Y_FM, PSB.rearrange("p (cc n) -> p cc n", n=TT)),
          reads=["bPSB", "Y_FMb"], writes=["Y_FMb"], tag="yTe")
        d("sync", lambda e, t=t: e.dma_start(
            out=ys[t].rearrange("(kc p) tt -> p kc tt", p=128), in_=Y_FM),
          reads=["Y_FMb"], writes=[("ys", t)], tag=f"yw{t}")

        def rmm(e):
            last = None
            for kc in range(8):
                last = e.matmul(PS2[:, 384:392], Y_FM[:, kc, :], gwT[:, kc, :],
                                start=(kc == 0), stop=(kc == 7))
            return last
        o("tensor", rmm, reads=["Y_FMb", "gwT", "bPS2"], writes=["bPS2"], tag="rt")
        o("vector", lambda e: e.tensor_copy(RT, PS2[:, 384:392]),
          reads=["bPS2", "RTb"], writes=["RTb"], tag="rte")
        tp = TOP
        A_, B_, Cc, Dd = tp[:, 2:10], tp[:, 10:18], tp[:, 18:26], tp[:, 26:34]
        o("vector", lambda e: e.reduce_max(tp[:, 0:1], RT, AX.X),
          reads=["RTb", "T0"], writes=["T0"], tag="t0")
        o("vector", lambda e: e.tensor_single_scalar(A_, RT, tp[:, 0:1], ALU.is_ge),
          reads=["RTb", "T0", "TA"], writes=["TA"], tag="t1")
        o("vector", lambda e: e.tensor_tensor_scan(
            B_, A_, ON1.broadcast_to([128, 8]), 0.0, ALU.add, ALU.mult),
          reads=["TA", "ON1", "TBk"], writes=["TBk"], tag="t2")
        o("vector", lambda e: e.tensor_single_scalar(Cc, B_, 1.0, ALU.is_equal),
          reads=["TBk", "TC"], writes=["TC"], tag="t3")
        o("vector", lambda e: e.tensor_tensor(B_, A_, Cc, ALU.mult),
          reads=["TA", "TC", "TBk"], writes=["TBk"], tag="t4")
        o("vector", lambda e: e.scalar_tensor_tensor(Cc, B_, -1e30, RT, ALU.mult, ALU.add),
          reads=["TBk", "RTb", "TC"], writes=["TC"], tag="t5")
        o("vector", lambda e: e.reduce_max(tp[:, 1:2], Cc, AX.X),
          reads=["TC", "T1"], writes=["T1"], tag="t6")
        o("vector", lambda e: e.tensor_single_scalar(A_, Cc, tp[:, 1:2], ALU.is_ge),
          reads=["TC", "T1", "TA"], writes=["TA"], tag="t7")
        o("vector", lambda e: e.tensor_tensor_scan(
            Dd, A_, ON1.broadcast_to([128, 8]), 0.0, ALU.add, ALU.mult),
          reads=["TA", "ON1", "TD"], writes=["TD"], tag="t8")
        o("vector", lambda e: e.tensor_single_scalar(Cc, Dd, 1.0, ALU.is_equal),
          reads=["TD", "TC"], writes=["TC"], tag="t9")
        o("vector", lambda e: e.tensor_tensor(Dd, A_, Cc, ALU.mult),
          reads=["TA", "TC", "TD"], writes=["TD"], tag="ta")
        o("vector", lambda e: e.tensor_sub(tp[:, 34:35], tp[:, 0:1], tp[:, 1:2]),
          reads=["T0", "T1", "Tdm"], writes=["Tdm"], tag="tb")
        o("scalar", lambda e: e.activation(tp[:, 35:36], tp[:, 34:35], AF.Sigmoid),
          reads=["Tdm", "Tw1"], writes=["Tw1"], tag="tc")
        o("vector", lambda e: e.tensor_scalar(
            tp[:, 36:37], tp[:, 35:36], -1.0, 1.0, ALU.mult, ALU.add),
          reads=["Tw1", "Tw2"], writes=["Tw2"], tag="td")
        o("vector", lambda e: e.tensor_scalar_mul(WFT, B_, tp[:, 35:36]),
          reads=["TBk", "Tw1", "WFTb"], writes=["WFTb"], tag="te")
        o("vector", lambda e: e.scalar_tensor_tensor(
            WFT, Dd, tp[:, 36:37], WFT, ALU.mult, ALU.add),
          reads=["TD", "Tw2", "WFTb"], writes=["WFTb"], tag="tf")
        o("tensor", lambda e: e.transpose(PS2[0:8, 384:512], WFT, IDF),
          reads=["WFTb", "IDF", "bPS2"], writes=["bPS2"], tag="tg")
        o("vector", lambda e: e.tensor_copy(WF8, PS2[0:8, 384:512]),
          reads=["bPS2", "WF8b"], writes=["WF8b"], tag="th")
        d("sync", lambda e, t0=t0: e.dma_start(out=ws[:, t0:t0+TT], in_=WF8),
          reads=["WF8b"], writes=[("ws", t)], tag=f"ww{t}")

        if DBG:
            ap = dbg["d_x2"][t0:t0+TT, :]
            d("gpsimd", lambda e, ap=ap: e.dma_start(out=ap, in_=X2T),
              reads=["X2Ta", "X2Tb", "OX2"], tag="tpx2")
            ap2 = dbg["d_y"][:, t0:t0+TT].rearrange("(kc p) tt -> p kc tt", p=128)
            d("gpsimd", lambda e, ap2=ap2: e.dma_start(out=ap2, in_=Y_FM),
              reads=["Y_FMb"], tag="tpy")
            ap3 = dbg["d_wf"][:, t0:t0+TT]
            d("gpsimd", lambda e, ap3=ap3: e.dma_start(out=ap3, in_=WF8),
              reads=["WF8b"], tag="tpwf")

    if BUILD_STAGE >= 3:
        kb.fence()
        d("gpsimd", lambda e: e.dma_start(out=W8, in_=ws),
          reads=[("ws", t) for t in range(NT)], writes=["W8"], tag="w8")
        for ei in range(NE):
            d("gpsimd", lambda e, ei=ei: e.dma_start(out=w1bf, in_=W1_in[ei]),
              reads=["w1bf"], writes=["w1bf"], tag=f"w1c{ei}")
            d("gpsimd", lambda e, ei=ei: e.dma_start(out=w2bf, in_=W2_in[ei]),
              reads=["w2bf"], writes=["w2bf"], tag=f"w2c{ei}")
            for cc in range(8):
                d("sync", lambda e, cc=cc: e.dma_start_transpose(
                    out=W1T[:, cc, :], in_=w1bf[:, cc*128:(cc+1)*128]),
                  reads=["w1bf", ("W1T", cc)], writes=[("W1T", cc)], tag=f"w1T{ei}_{cc}")
            for hc in range(32):
                d("sync", lambda e, hc=hc: e.dma_start_transpose(
                    out=W2T[:, hc, :], in_=w2bf[:, hc*128:(hc+1)*128]),
                  reads=["w2bf", ("W2T", hc)], writes=[("W2T", hc)], tag=f"w2T{ei}_{hc}")
            d("sync", lambda e, ei=ei: e.dma_start(out=B1C, in_=b1_in[ei]),
              reads=["B1C"], writes=["B1C"], tag=f"b1{ei}")
            d("sync", lambda e, ei=ei: e.dma_start(out=B2C, in_=b2_in[ei]),
              reads=["B2C"], writes=["B2C"], tag=f"b2{ei}")
            for bt in range(NTB):
                tb = bt % 2
                bt0 = bt * TB
                for bb in range(2):
                    d("sync", lambda e, bt=bt, tb=tb, bb=bb: e.dma_start(
                        out=YTL[:, tb, :, bb*TT:(bb+1)*TT],
                        in_=ys[2*bt+bb].rearrange("(kc p) n -> p kc n", p=128)),
                      reads=[("ys", 2*bt+bb), ("YTL", tb)],
                      writes=[("YTL", tb)], tag=f"ylo{ei}_{bt}_{bb}")
                o("tensor", lambda e, ei=ei, bt0=bt0: e.matmul(
                    PS2[:, TB:2*TB], E8S[:, ei*128:(ei+1)*128],
                    W8[:, bt0:bt0+TB], start=True, stop=True),
                  reads=["W8", "E8S", "bPS2"], writes=["bPS2"], tag=f"wb{ei}_{bt}")
                o("vector", lambda e: e.tensor_copy(WB, PS2[:, TB:2*TB]),
                  reads=["bPS2", "WBb"], writes=["WBb"], tag=f"we{ei}_{bt}")
                for hc in range(32):
                    ps, pk = (PS0, "bPS0") if hc % 2 == 0 else (PS1, "bPS1")
                    def mm1(e, hc=hc, tb=tb, ps=ps):
                        last = None
                        for kc in range(8):
                            last = e.matmul(ps[:, 0:TB], W1T[:, kc, hc*128:(hc+1)*128],
                                            YTL[:, tb, kc], start=(kc == 0), stop=(kc == 7))
                        return last
                    o("tensor", mm1,
                      reads=[("W1T", kc) for kc in range(8)] + [("YTL", tb), pk],
                      writes=[pk], tag=f"m1_{ei}_{bt}_{hc}")
                    o("scalar", lambda e, hc=hc, ps=ps: e.activation(
                        HT[:, hc, :], ps[:, 0:TB], AF.Relu, bias=B1C[:, hc:hc+1]),
                      reads=[pk, "B1C", ("HT", hc)], writes=[("HT", hc)],
                      tag=f"rl_{ei}_{bt}_{hc}")
                for oc in range(8):
                    ps, pk = (PS4, "bPS4") if oc % 2 == 0 else (PS5, "bPS5")
                    def mm2(e, oc=oc, ps=ps):
                        last = None
                        for hc in range(32):
                            last = e.matmul(ps[:, 0:TB], W2T[:, hc, oc*128:(oc+1)*128],
                                            HT[:, hc, :], start=(hc == 0), stop=(hc == 31))
                        return last
                    o("tensor", mm2,
                      reads=[("W2T", hc) for hc in range(32)] + [("HT", hc) for hc in range(32)] + [pk],
                      writes=[pk], tag=f"m2_{ei}_{bt}_{oc}")
                    if ei == NE - 1:
                        d("sync", lambda e, bt=bt, oc=oc, tb=tb: e.dma_start(
                            out=ACC1[:, tb], in_=accs[bt, oc*128:(oc+1)*128, :]),
                          reads=[("accs", bt), ("ACC1", tb)], writes=[("ACC1", tb)],
                          tag=f"a1_{bt}_{oc}")
                    o("vector", lambda e, oc=oc, ps=ps, tb=tb: e.scalar_tensor_tensor(
                        ACCN[:, tb, oc], ps[:, 0:TB], B2C[:, oc:oc+1], WB,
                        ALU.add, ALU.mult),
                      reads=[pk, "B2C", "WBb", ("ACCN", tb, oc)],
                      writes=[("ACCN", tb, oc)], tag=f"e1_{ei}_{bt}_{oc}")
                    if ei == NE - 1:
                        o("vector", lambda e, oc=oc, tb=tb: e.tensor_add(
                            ACCN[:, tb, oc], ACCN[:, tb, oc], ACC1[:, tb]),
                          reads=[("ACCN", tb, oc), ("ACC1", tb)],
                          writes=[("ACCN", tb, oc)], tag=f"e2_{bt}_{oc}")
                if ei == 0:
                    d("sync", lambda e, bt=bt, tb=tb: e.dma_start(
                        out=accs[bt].rearrange("(kc p) tt -> p kc tt", p=128),
                        in_=ACCN[:, tb]),
                      reads=[("ACCN", tb, oc) for oc in range(8)], writes=[("accs", bt)],
                      tag=f"aw{ei}_{bt}")
                elif ei < NE - 1:
                    d("gpsimd", lambda e, bt=bt, tb=tb: e.dma_start(
                        out=accs[bt].rearrange("(kc p) tt -> p kc tt", p=128),
                        in_=ACCN[:, tb], accum_op=ALU.add),
                      reads=[("ACCN", tb, oc) for oc in range(8)] + [("accs", bt)],
                      writes=[("accs", bt)], tag=f"aw{ei}_{bt}")
                else:
                    d("sync", lambda e, bt=bt, bt0=bt0: e.dma_start(
                        out=X2L, in_=x2s[bt0:bt0+TB, :].rearrange("(q p) c -> p q c", p=128)),
                      reads=[("x2s", bt), "X2Lb"], writes=["X2Lb"], tag=f"xlB{bt}")
                    for q in range(2):
                        def ftr(e, q=q, tb=tb):
                            last = None
                            for cc in range(4):
                                last = e.transpose(PS0[:, cc*128:(cc+1)*128],
                                                   ACCN[:, tb, cc, q*128:(q+1)*128], IDF)
                            for cc in range(4, 8):
                                last = e.transpose(PS1[:, (cc-4)*128:(cc-3)*128],
                                                   ACCN[:, tb, cc, q*128:(q+1)*128], IDF)
                            return last
                        o("tensor", ftr,
                          reads=[("ACCN", tb, cc) for cc in range(8)] + ["IDF", "bPS0", "bPS1"],
                          writes=["bPS0", "bPS1"], tag=f"ft{bt}_{q}")
                        o("vector", lambda e, q=q: e.tensor_add(
                            OUTST[:, q, 0:512], PS0[:, 0:512], X2L[:, q, 0:512]),
                          reads=["bPS0", "X2Lb", ("OUTST", q)], writes=[("OUTST", q)],
                          tag=f"f1{bt}_{q}")
                        o("vector", lambda e, q=q: e.tensor_add(
                            OUTST[:, q, 512:1024], PS1[:, 0:512], X2L[:, q, 512:1024]),
                          reads=["bPS1", "X2Lb", ("OUTST", q)], writes=[("OUTST", q)],
                          tag=f"f2{bt}_{q}")
                    d("sync", lambda e, bt0=bt0: e.dma_start(
                        out=out_t[bt0:bt0+TB, :].rearrange("(q p) c -> p q c", p=128),
                        in_=OUTST),
                      reads=[("OUTST", 0), ("OUTST", 1)], writes=[("out", bt)],
                      tag=f"os{bt}")

    kb.finalize_and_emit()
    return nc


_NC_CACHE = None


def kernel(**inputs):
    global _NC_CACHE
    consts = host_consts()
    if _NC_CACHE is None:
        _NC_CACHE = build_program()
    nc = _NC_CACHE
    x = np.ascontiguousarray(np.asarray(inputs["L0_attn_hidden_states"], np.float32))
    gi = lambda n: np.ascontiguousarray(np.asarray(inputs[n], np.float32))
    base = {
        "W_r": gi("L0_attn_W_r"), "W_k": gi("L0_attn_W_k"),
        "W_v": gi("L0_attn_W_v"), "W_o": gi("L0_attn_W_o"),
        "w_lora_a": gi("L0_attn_w_lora_a"), "w_lora_b": gi("L0_attn_w_lora_b"),
        "a_lora_a": gi("L0_attn_a_lora_a"), "a_lora_b": gi("L0_attn_a_lora_b"),
        "g_lora_a": gi("L0_attn_g_lora_a"), "g_lora_b": gi("L0_attn_g_lora_b"),
        "gate_w": gi("L0_ffn_gate_w"),
        "W1": gi("L0_ffn_W1"), "W2": gi("L0_ffn_W2"),
    }
    vecs = np.stack([gi("L0_attn_" + n).reshape(8, 128).T for n in VEC_NAMES], axis=-1)
    base["vecs"] = np.ascontiguousarray(vecs)
    base["r_k"] = np.ascontiguousarray(gi("L0_attn_r_k").reshape(-1).reshape(8, 128).T)
    base["b1"] = np.ascontiguousarray(gi("L0_ffn_b1").reshape(NE, 32, 128).transpose(0, 2, 1))
    base["b2"] = np.ascontiguousarray(gi("L0_ffn_b2").reshape(NE, 8, 128).transpose(0, 2, 1))
    base.update(consts)
    in_maps = [dict(base, x=np.ascontiguousarray(x[b])) for b in range(8)]
    res = run_bass_kernel_spmd(nc, in_maps, core_ids=list(range(8)))
    out = np.stack([np.asarray(r["out"], np.float32) for r in res.results], axis=0)
    if DBG:
        kernel.last_results = res.results
    return out


# version 05 (final)
# speedup vs baseline: 1.0832x; 1.0832x over previous
import numpy as np
import ml_dtypes
import concourse.bass as bass
import concourse.mybir as mybir
from concourse.bass_utils import run_bass_kernel_spmd

F32 = mybir.dt.float32
BF16 = mybir.dt.bfloat16
AF = mybir.ActivationFunctionType
ALU = mybir.AluOpType
AX = mybir.AxisListType
bf16 = ml_dtypes.bfloat16

DBG = False
BUILD_STAGE = 3
ASTAGE = 99

T_TOT = 4096
C = 1024
TT = 128
NT = T_TOT // TT
TB = 256
NTB = T_TOT // TB
NH = 16
HD = 64
LCH = 64
NE = 8
HID = 4096


class _Op:
    __slots__ = ("eng", "fn", "deps", "inc", "val", "dma", "waits", "tag")

    def __init__(self, eng, fn, dma, tag):
        self.eng, self.fn, self.dma, self.tag = eng, fn, dma, tag
        self.deps = []
        self.inc = False
        self.val = 0
        self.waits = []


class KB:
    ENGS = ("tensor", "vector", "scalar", "gpsimd", "sync")

    def __init__(self, nc):
        self.nc = nc
        self.ops = []
        self.last_w = {}
        self.readers = {}
        self._fence_deps = []
        self._dma_rr = {"qsync": 0, "qpool": 0}
        self._last_dma = {}

    def op(self, eng, fn, reads=(), writes=(), tag=""):
        return self._add(eng, fn, reads, writes, None, tag)

    def dma(self, eng, fn, reads=(), writes=(), tag=""):
        q = "qpool" if eng == "gpsimd" else "qsync"
        slot = self._dma_rr[q] % 8
        self._dma_rr[q] += 1
        dom = f"{q}{slot}"
        idx = self._add(eng, fn, reads, writes, dom, tag)
        prev = self._last_dma.get(dom)
        if prev is not None and prev not in self.ops[idx].deps:
            self.ops[idx].deps.append(prev)
            self.ops[idx].deps.sort()
        self._last_dma[dom] = idx
        return idx

    def fence(self):
        last_by_dom = {}
        for i, op in enumerate(self.ops):
            last_by_dom[op.dma if op.dma else op.eng] = i
        self._fence_deps = sorted(last_by_dom.values())

    def _add(self, eng, fn, reads, writes, dma, tag):
        op = _Op(eng, fn, dma, tag)
        idx = len(self.ops)
        deps = set(self._fence_deps)
        for b in reads:
            t = self.last_w.get(b)
            if t is not None:
                deps.add(t)
        for b in writes:
            t = self.last_w.get(b)
            if t is not None:
                deps.add(t)
            for t2 in self.readers.get(b, ()):
                deps.add(t2)
        deps.discard(idx)
        op.deps = sorted(deps)
        self.ops.append(op)
        for b in reads:
            self.readers.setdefault(b, []).append(idx)
        for b in writes:
            self.last_w[b] = idx
            self.readers[b] = []
        return idx

    def finalize_and_emit(self):
        nc = self.nc
        for op in self.ops:
            for dd in op.deps:
                self.ops[dd].inc = True
        for op in self.ops:
            if op.dma:
                op.inc = True
        counters = {}
        for op in self.ops:
            if op.inc:
                dom = op.dma if op.dma else op.eng
                counters[dom] = counters.get(dom, 0) + (16 if op.dma else 1)
                op.val = counters[dom]
        for op in self.ops:
            w = {}
            for dd in op.deps:
                p = self.ops[dd]
                dom = p.dma if p.dma else p.eng
                w[dom] = max(w.get(dom, 0), p.val)
            op.waits = w
        hw = {e: {} for e in self.ENGS}
        for op in self.ops:
            pruned = {}
            h = hw[op.eng]
            for dom, v in op.waits.items():
                if h.get(dom, 0) < v:
                    pruned[dom] = v
                    h[dom] = v
            op.waits = sorted(pruned.items())
        sems = {}
        for dom in set(counters) | set(self.ENGS):
            sems[dom] = nc.alloc_semaphore(f"sem_{dom}")
        per_eng = {e: [op for op in self.ops if op.eng == e] for e in self.ENGS}
        final_waits = sorted(counters.items())

        with nc.allow_low_precision(reason="deliberate bf16 operands; fp32 accumulation on all matmuls"), nc.Block() as block:
            def make_body(engname, ops):
                def body(eng):
                    for op in ops:
                        for dom, v in op.waits:
                            eng.wait_ge(sems[dom], v)
                        ins = op.fn(eng)
                        if op.inc:
                            assert ins is not None, f"op {op.tag} returned None"
                            dom = op.dma if op.dma else op.eng
                            ins.then_inc(sems[dom], 16 if op.dma else 1)
                    if engname == "sync":
                        for dom, v in final_waits:
                            eng.wait_ge(sems[dom], v)
                return body
            for e in self.ENGS:
                getattr(block, e)(make_body(e, per_eng[e]))


def host_consts():
    ID_BF = np.eye(128, dtype=bf16)
    ID_F32 = np.eye(128, dtype=np.float32)
    tl = np.tril(np.ones((LCH, LCH), np.float32), -1)
    tu = np.triu(np.ones((LCH, LCH), np.float32), 1)
    ti = np.triu(np.ones((LCH, LCH), np.float32), 0)
    z = np.zeros_like(tl)
    bd = lambda m: np.block([[m, z], [z, m]]).astype(np.float32)
    MASKS = np.concatenate([bd(tl), bd(tu), bd(ti)], axis=1).astype(np.uint8)
    E8 = np.zeros((8, 8 * 128), dtype=bf16)
    for e in range(8):
        E8[e, e * 128:(e + 1) * 128] = 1
    EXP2 = np.zeros((2, 128), dtype=bf16)
    EXP2[0, 0:64] = 1
    EXP2[1, 64:128] = 1
    ONES2 = np.zeros((128, 2), dtype=bf16)
    ONES2[0:64, 0] = 1
    ONES2[64:128, 1] = 1
    ONES1 = np.ones((128, 1), np.float32)
    return {"c_idbf": ID_BF, "c_idf32": ID_F32, "c_masks": MASKS,
            "c_e8": E8, "c_exp2": EXP2, "c_ones2": ONES2, "c_ones1": ONES1,
            "c_misc": np.concatenate([np.ones((128, 1), np.float32),
                                      np.full((128, 1), 1e-5, np.float32)], axis=1)}


VEC_NAMES = ["x_r", "x_w", "x_k", "x_v", "x_a", "k_k", "k_a",
             "w_lora_bias", "a_lora_bias", "g_lora_bias", "gn_weight", "gn_bias"]
VIX = {n: i for i, n in enumerate(VEC_NAMES)}


def build_program():
    nc = bass.Bass("TRN2", target_bir_lowering=False)
    kb = KB(nc)
    o = kb.op
    d = kb.dma

    EI = lambda n, s, dt=F32: nc.dram_tensor(n, s, dt, kind="ExternalInput").ap()
    x_in = EI("x", [T_TOT, C])
    out_t = nc.dram_tensor("out", [T_TOT, C], F32, kind="ExternalOutput").ap()
    vecs_in = EI("vecs", [128, 8, 12])
    rk_in = EI("r_k", [128, 8])
    W_r = EI("W_r", [C, C]); W_k = EI("W_k", [C, C])
    W_v = EI("W_v", [C, C]); W_o = EI("W_o", [C, C])
    wla_in = EI("w_lora_a", [C, 64]); wlb_in = EI("w_lora_b", [64, C])
    ala_in = EI("a_lora_a", [C, 64]); alb_in = EI("a_lora_b", [64, C])
    gla_in = EI("g_lora_a", [C, 128]); glb_in = EI("g_lora_b", [128, C])
    gw_in = EI("gate_w", [8, C])
    W1_in = EI("W1", [NE, HID, C]); b1_in = EI("b1", [NE, 128, 32])
    W2_in = EI("W2", [NE, C, HID]); b2_in = EI("b2", [NE, 128, 8])
    c_idbf = EI("c_idbf", [128, 128], BF16)
    c_idf32 = EI("c_idf32", [128, 128], F32)
    c_masks = EI("c_masks", [128, 384], mybir.dt.uint8)
    c_e8 = EI("c_e8", [8, 8 * 128], BF16)
    c_exp2 = EI("c_exp2", [2, 128], BF16)
    c_ones2 = EI("c_ones2", [128, 2], BF16)
    c_ones1 = EI("c_ones1", [128, 1], F32)
    c_misc = EI("c_misc", [128, 2], F32)

    x2s = nc.dram_tensor("x2s", [T_TOT, C], F32, kind="Internal").ap()
    ys = nc.dram_tensor("ys", [NT, C, TT], BF16, kind="Internal").ap()
    ws = nc.dram_tensor("ws", [8, NT * TT], F32, kind="Internal").ap()
    accs = nc.dram_tensor("accs", [NTB, C, TB], F32, kind="Internal").ap()
    w1bf = nc.dram_tensor("w1bf", [HID, C], BF16, kind="Internal").ap()
    w2bf = nc.dram_tensor("w2bf", [C, HID], BF16, kind="Internal").ap()

    dbg = {}
    if DBG:
        for n, shape in [("d_hs", [C, T_TOT]), ("d_r", [C, T_TOT]),
                         ("d_logw", [C, T_TOT]), ("d_k2", [C, T_TOT]),
                         ("d_a", [C, T_TOT]), ("d_kk", [C, T_TOT]),
                         ("d_g", [C, T_TOT]), ("d_v", [C, T_TOT]),
                         ("d_o", [T_TOT, C]), ("d_of", [C, T_TOT]),
                         ("d_x2", [T_TOT, C]), ("d_y", [C, T_TOT]),
                         ("d_wf", [8, T_TOT])]:
            dbg[n] = nc.dram_tensor(n, shape, F32, kind="ExternalOutput").ap()

    sb = lambda name, shape, dt: nc.alloc_sbuf_tensor(name, shape, dt).ap()

    IDB = sb("IDB", [128, 128], BF16)
    IDF = sb("IDF", [128, 128], F32)
    MSK = sb("MSK", [128, 384], mybir.dt.uint8)
    E8S = sb("E8S", [8, 8 * 128], BF16)
    EX2 = sb("EX2", [2, 128], BF16)
    ON2 = sb("ON2", [128, 2], BF16)
    ON1 = sb("ON1", [128, 1], F32)
    MISC = sb("MISC", [128, 2], F32)
    ZRO = sb("ZRO", [128, 1], F32)
    VEC = sb("VEC", [128, 8, 12], F32)
    NEG = sb("NEG", [128, 8, 6], F32)
    rkv = sb("rkv", [128, 8], F32)
    rk2 = sb("rk2", [128, 8, 2], BF16)

    arena_base = nc.sbuf_base

    Wr = sb("Wr", [128, 8, C], BF16); Wk = sb("Wk", [128, 8, C], BF16)
    Wv = sb("Wv", [128, 8, C], BF16); Wo = sb("Wo", [128, 8, C], BF16)
    wla = sb("wla", [128, 8, 64], BF16); wlb = sb("wlb", [64, C], BF16)
    ala = sb("ala", [128, 8, 64], BF16); alb = sb("alb", [64, C], BF16)
    gla = sb("gla", [128, 8, 128], BF16); glb = sb("glb", [128, C], BF16)
    gwn = sb("gwn", [8, C], BF16)
    gwT = sb("gwT", [128, 8, 8], BF16)
    XT = sb("XT", [128, 2, C], F32)
    DUMP = sb("DUMP", [128, C], F32)
    STT = sb("STT", [128, 8], F32)
    XHC = sb("XHC", [128, C], BF16)
    XH_FM = sb("XH_FM", [128, 8, TT + 2], BF16)
    TMPM = sb("TMPM", [128, 8, TT], BF16)
    MIX = sb("MIX", [128, 2, 8, TT], BF16)
    R_FM = sb("R_FM", [128, 8, TT], BF16)
    K_FM = sb("K_FM", [128, 8, TT], BF16)
    V_FM = sb("V_FM", [128, 8, TT], BF16)
    KKR = sb("KKR", [128, 8, TT], BF16)
    KK_FM = sb("KK_FM", [128, 8, TT], BF16)
    A_FM = sb("A_FM", [128, 8, TT], BF16)
    K2_FM = sb("K2_FM", [128, 8, TT], BF16)
    SCR = sb("SCR", [128, 8, TT], F32)
    SC2 = sb("SC2", [2, 3, TT], F32)
    SCB = sb("SCB", [128, 8, TT], BF16)
    LOGW = sb("LOGW", [128, 8, TT], F32)
    CG = sb("CG", [128, 8, TT + 1], F32)
    CB = sb("CB", [128, 8, 3], F32)
    PMID = sb("PMID", [128, 8, 2], F32)
    ECL = sb("ECL", [128, 8, 2], F32)
    EPHI = sb("EPHI", [128, 8, 2], F32)
    FRT = sb("FRT", [128, 8, TT], BF16)
    FKB = sb("FKB", [128, 8, TT], BF16)
    FSH = sb("FSH", [128, 8, TT], BF16)
    SAR = sb("SAR", [128, 8, 2, TT], BF16)
    SB_ = sb("SB_", [128, 8, TT], BF16)
    SK_ = sb("SK_", [128, 8, TT], BF16)
    SAH = sb("SAH", [128, 8, TT], BF16)
    SRH = sb("SRH", [128, 8, TT], BF16)
    TMX = sb("TMX", [128, 2, 192], BF16)
    YST = sb("YST", [128, 2, 2, 128], BF16)
    MPL = sb("MPL", [128, 2, 2, 128], BF16)
    MTR = sb("MTR", [128, 2, 2, 128], BF16)
    CTM = sb("CTM", [128, 2, 128], BF16)
    BRU = sb("BRU", [128, 2, 128], BF16)
    KRU = sb("KRU", [128, 2, 128], BF16)
    ZTP = sb("ZTP", [128, 4, 128], BF16)
    WPD = sb("WPD", [128, 4, 2, 64], BF16)
    YLC = sb("YLC", [128, 2, 64], F32)
    SST = sb("SST", [128, NH, 2, HD], BF16)
    STMP = sb("STMP", [128, 64], F32)
    OTM = sb("OTM", [128, C], F32)
    MU = sb("MU", [128, NH], F32)
    SSQ = sb("SSQ", [128, NH], F32)
    RSTD = sb("RSTD", [128, NH], F32)
    MRS = sb("MRS", [128, NH], F32)
    ON_FM = sb("ON_FM", [128, 8, TT], BF16)
    SBC = sb("SBC", [2, 8, TT], BF16)
    OG = sb("OG", [128, 8, TT], BF16)
    Y_FM = sb("Y_FM", [128, 8, TT], BF16)
    RT = sb("RT", [128, 8], F32)
    TOP = sb("TOP", [128, 40], F32)
    WFT = sb("WFT", [128, 8], F32)
    WF8 = sb("WF8", [8, TT], F32)
    X2T = OTM
    XH_TM = XHC
    ON_TM = XHC
    Y_TM = XHC
    G_FM = KKR
    PRK = K_FM

    assert nc.sbuf_base < nc.sbuf_top, "phase A SBUF overflow"

    boff = [arena_base]

    def sbB(name, shape, dt):
        per = int(np.prod(shape[1:])) * mybir.dt.size(dt)
        per = (per + 31) // 32 * 32
        h = nc.alloc_sbuf_tensor_at(name, shape, dt, offset=boff[0])
        boff[0] += per
        assert boff[0] <= nc.sbuf_top, f"phase B overflow at {name}"
        return h.ap()

    W1T = sbB("W1T", [128, 8, HID], BF16)
    W2T = sbB("W2T", [128, 32, C], BF16)
    HT = sbB("HT", [128, 32, TB], BF16)
    YTL = sbB("YTL", [128, 2, 8, TB], BF16)
    ACCN = sbB("ACCN", [128, 2, 8, TB], F32)
    ACC1 = sbB("ACC1", [128, 2, TB], F32)
    X2L = sbB("X2L", [128, 2, C], F32)
    OUTST = sbB("OUTST", [128, 2, C], F32)
    W8 = sbB("W8", [8, NT * TT], BF16)
    WB = sbB("WB", [128, TB], F32)
    B1C = sbB("B1C", [128, 32], F32)
    B2C = sbB("B2C", [128, 8], F32)

    psum = lambda name, shape, dt=F32: nc.alloc_psum_tensor(name, shape, dt).ap()
    PS0 = psum("PS0", [128, 512])
    PS1 = psum("PS1", [128, 512])
    PS2 = psum("PS2", [128, 512])
    PS3 = psum("PS3", [128, 512])
    PS4 = psum("PS4", [128, 512])
    PS5 = psum("PS5", [128, 512])
    PS6 = psum("PS6", [128, 512])
    PSB = psum("PSB", [128, 1024], BF16)

    maskA = MSK[:, 0:128]
    maskB = MSK[:, 128:256]
    maskC = MSK[:, 256:384]
    Z128 = ZRO.broadcast_to([128, 128])

    def vcol(n, cc):
        return VEC[:, cc, VIX[n]:VIX[n] + 1]

    d("sync", lambda e: e.dma_start(out=IDB, in_=c_idbf), writes=["IDB"])
    d("sync", lambda e: e.dma_start(out=IDF, in_=c_idf32), writes=["IDF"])
    d("sync", lambda e: e.dma_start(out=MSK, in_=c_masks), writes=["MSK"])
    d("sync", lambda e: e.dma_start(out=E8S, in_=c_e8), writes=["E8S"])
    d("sync", lambda e: e.dma_start(out=EX2, in_=c_exp2), writes=["EX2"])
    d("sync", lambda e: e.dma_start(out=ON2, in_=c_ones2), writes=["ON2"])
    d("sync", lambda e: e.dma_start(out=ON1, in_=c_ones1), writes=["ON1"])
    d("sync", lambda e: e.dma_start(out=MISC, in_=c_misc), writes=["MISC"])
    nc.const_aps.register(F32, 1.0, MISC[:, 0:1])
    nc.const_aps.register(F32, 1e-5, MISC[:, 1:2])
    d("sync", lambda e: e.dma_start(out=VEC, in_=vecs_in),
      writes=[("VEC", i) for i in range(12)])
    d("sync", lambda e: e.dma_start(out=rkv, in_=rk_in), writes=["rkv"])
    for nm, dst, src in [("Wr", Wr, W_r), ("Wk", Wk, W_k), ("Wv", Wv, W_v),
                         ("Wo", Wo, W_o), ("wla", wla, wla_in),
                         ("ala", ala, ala_in), ("gla", gla, gla_in)]:
        ap = src.rearrange("(kc p) n -> p kc n", p=128)
        d("gpsimd", lambda e, dst=dst, ap=ap: e.dma_start(out=dst, in_=ap),
          writes=[nm])
    for nm, dst, src in [("wlb", wlb, wlb_in), ("alb", alb, alb_in),
                         ("glb", glb, glb_in), ("gwn", gwn, gw_in)]:
        d("gpsimd", lambda e, dst=dst, src=src: e.dma_start(out=dst, in_=src),
          writes=[nm])

    o("vector", lambda e: e.memset(ZRO, 0.0), writes=["ZRO"])
    for i in range(5):
        o("vector", lambda e, i=i: e.tensor_scalar(
            NEG[:, :, i], VEC[:, :, i], -1.0, 1.0, ALU.mult, ALU.add),
          reads=[("VEC", i)], writes=[("NEG", i)])
    o("vector", lambda e: e.tensor_scalar(
        NEG[:, :, 5], VEC[:, :, VIX["k_a"]], -1.0, 1.0, ALU.mult, ALU.add),
      reads=[("VEC", VIX["k_a"])], writes=[("NEG", 5)])
    o("vector", lambda e: e.memset(rk2, 0.0), writes=[("rk2", cc) for cc in range(8)])
    for cc in range(8):
        o("scalar", lambda e, cc=cc: e.copy(rk2[0:64, cc, 0:1], rkv[0:64, cc:cc+1]),
          reads=["rkv"], writes=[("rk2", cc)])
        o("scalar", lambda e, cc=cc: e.copy(rk2[64:128, cc, 1:2], rkv[64:128, cc:cc+1]),
          reads=["rkv"], writes=[("rk2", cc)])
    def gwt_t(e):
        last = None
        for cc in range(8):
            last = e.transpose(PSB[0:128, cc*8:(cc+1)*8], gwn[:, cc*128:(cc+1)*128], IDB[0:8, 0:8])
        return last
    o("tensor", gwt_t, reads=["gwn", "IDB", "bPSB"], writes=["bPSB"], tag="gwT")
    o("scalar", lambda e: e.copy(gwT, PSB[:, 0:64].rearrange("p (cc ee) -> p cc ee", ee=8)),
      reads=["bPSB"], writes=["gwT"], tag="gwTe")
    o("vector", lambda e: e.memset(XH_FM[:, :, 0:1], 0.0), writes=["XH_halo"])
    o("vector", lambda e: e.memset(CG, 0.0), writes=[("CG", cc) for cc in range(8)])
    o("vector", lambda e: e.memset(SST, 0.0),
      writes=[("SST", hh, bu) for hh in range(NH) for bu in range(2)])
    o("vector", lambda e: e.memset(ZTP, 0.0), writes=[("ZTP", i) for i in range(4)])
    o("vector", lambda e: e.memset(WPD, 0.0), writes=[("WPD", i, c2) for i in range(4) for c2 in range(2)])

    def emit_ln(src_ap, srckeys, out_ap, outkey, tag):
        st = STT
        o("scalar", lambda e: e.activation(DUMP, src_ap, AF.Identity,
                                           accum_out=st[:, 0:1]),
          reads=srckeys + ["DUMP"], writes=["STs", "DUMP"], tag=tag + "sum")
        o("scalar", lambda e: e.activation(DUMP, src_ap, AF.Square,
                                           accum_out=st[:, 1:2]),
          reads=srckeys + ["STs", "DUMP"], writes=["STq", "DUMP"], tag=tag + "ssq")
        o("vector", lambda e: e.tensor_scalar_mul(st[:, 2:3], st[:, 0:1], 1.0 / C),
          reads=["STs"], writes=["STm"], tag=tag + "m")
        o("vector", lambda e: e.tensor_tensor(st[:, 3:4], st[:, 2:3], st[:, 2:3], ALU.mult),
          reads=["STm"], writes=["STm2"], tag=tag + "m2")
        o("vector", lambda e: e.scalar_tensor_tensor(
            st[:, 4:5], st[:, 1:2], 1.0 / C, st[:, 3:4], ALU.mult, ALU.subtract),
          reads=["STq", "STm2"], writes=["STv"], tag=tag + "v")
        o("scalar", lambda e: e.activation(st[:, 5:6], st[:, 4:5], AF.Sqrt, bias=1e-5),
          reads=["STv", "MISC"], writes=["STsd"], tag=tag + "sd")
        o("vector", lambda e: e.reciprocal(st[:, 6:7], st[:, 5:6]),
          reads=["STsd"], writes=["STr"], tag=tag + "r")
        o("vector", lambda e: e.tensor_tensor(st[:, 7:8], st[:, 2:3], st[:, 6:7], ALU.mult),
          reads=["STm", "STr"], writes=["STmr"], tag=tag + "mr")
        o("vector", lambda e: e.tensor_scalar(
            out_ap, src_ap, st[:, 6:7], st[:, 7:8], ALU.mult, ALU.subtract),
          reads=srckeys + ["STr", "STmr", outkey], writes=[outkey], tag=tag + "ap")

    for t in range(NT if BUILD_STAGE >= 1 else 0):
        tb = t % 2
        t0 = t * TT
        gchunk0 = t * 2
        d("sync", lambda e, t0=t0, tb=tb: e.dma_start(
            out=XT[:, tb], in_=x_in[t0:t0 + TT, :]),
          reads=[("XT", tb)], writes=[("XT", tb)], tag=f"xl{t}")

        emit_ln(XT[:, tb], [("XT", tb)], XH_TM, "XHC", f"ln1_{t}")

        if t > 0:
            o("vector", lambda e: e.tensor_copy(XH_FM[:, :, 0:1], XH_FM[:, :, TT:TT+1]),
              reads=[("XH_FM",)], writes=["XH_halo"], tag=f"halo{t}")
        def xh_t(e):
            last = None
            for cc in range(8):
                last = e.transpose(PSB[:, cc*128:(cc+1)*128],
                                   XH_TM[:, cc*128:(cc+1)*128], IDB)
            return last
        o("tensor", xh_t, reads=["XHC", "IDB", "bPSB"], writes=["bPSB"],
          tag=f"xhT{t}")
        o("scalar", lambda e: e.copy(
            XH_FM[:, :, 1:TT+1], PSB.rearrange("p (cc n) -> p cc n", n=TT)),
          reads=["bPSB", "XH_halo", ("XH_FM",)], writes=[("XH_FM",)], tag=f"xhE{t}")

        if ASTAGE < 2:
            continue

        def emit_mix(mi, name):
            mb = mi % 2
            xf = [("XH_FM",), "XH_halo"]
            for cc in range(8):
                o("scalar", lambda e, cc=cc: e.activation(
                    TMPM[:, cc], XH_FM[:, cc, 0:TT], AF.Identity, scale=vcol("x_" + name, cc)),
                  reads=xf + [("VEC", VIX["x_" + name]), ("TMPM", cc)],
                  writes=[("TMPM", cc)], tag=f"ms{name}{cc}")
                o("vector", lambda e, cc=cc, mb=mb: e.scalar_tensor_tensor(
                    MIX[:, mb, cc], XH_FM[:, cc, 1:TT+1],
                    NEG[:, cc, VIX["x_" + name]:VIX["x_" + name]+1],
                    TMPM[:, cc], ALU.mult, ALU.add),
                  reads=xf + [("TMPM", cc), ("NEG", VIX["x_" + name]), ("MIX", mb, cc)],
                  writes=[("MIX", mb, cc)], tag=f"mv{name}{cc}")
            return mb

        def proj_1024(mb, Wsb, wkey, dstbuf, okey, extrakeys=()):
            for oc in range(8):
                ps, pk = (PS0, "bPS0") if oc % 2 == 0 else (PS1, "bPS1")
                def mm(e, oc=oc, mb=mb, Wsb=Wsb, ps=ps):
                    last = None
                    for kc in range(8):
                        last = e.matmul(ps[:, 0:TT], Wsb[:, kc, oc*128:(oc+1)*128],
                                        MIX[:, mb, kc], start=(kc == 0), stop=(kc == 7))
                    return last
                o("tensor", mm, reads=[wkey, pk] + [("MIX", mb, kc) for kc in range(8)],
                  writes=[pk], tag=f"pj{okey}{oc}")
                o("scalar", lambda e, oc=oc, ps=ps, dstbuf=dstbuf: e.copy(
                    dstbuf[:, oc], ps[:, 0:TT]),
                  reads=[pk, (okey, oc)], writes=[(okey, oc)] + list(extrakeys),
                  tag=f"pe{okey}{oc}")

        mb = emit_mix(0, "r")
        proj_1024(mb, Wr, "Wr", R_FM, "R_FM")

        if ASTAGE < 3:
            continue
        mb = emit_mix(1, "w")
        def wl_mm(e, mb=mb):
            last = None
            for kc in range(8):
                last = e.matmul(PS2[0:64, 0:TT], wla[:, kc, :], MIX[:, mb, kc],
                                start=(kc == 0), stop=(kc == 7))
            return last
        o("tensor", wl_mm, reads=["wla", "bPS2"] + [("MIX", mb, kc) for kc in range(8)],
          writes=["bPS2"], tag="wlA")
        if ASTAGE == 31:
            continue
        TNH_ap = SCB[0:64, 0, :]
        o("scalar", lambda e: e.activation(TNH_ap, PS2[0:64, 0:TT], AF.Tanh),
          reads=["bPS2", ("SCR", 0)], writes=[("SCR", 0)], tag="wlT")
        if ASTAGE == 315:
            continue
        for oc in range(8):
            ps, pk = (PS2, "bPS2") if oc % 2 == 0 else (PS3, "bPS3")
            o("tensor", lambda e, oc=oc, ps=ps: e.matmul(
                ps[:, TT:2*TT], wlb[:, oc*128:(oc+1)*128], TNH_ap, start=True, stop=True),
              reads=["wlb", ("SCR", 0), pk], writes=[pk], tag=f"wlB{oc}")
            if ASTAGE == 32:
                o("scalar", lambda e, oc=oc, ps=ps: e.copy(LOGW[:, oc], ps[:, TT:2*TT]),
                  reads=[pk, ("LOGW", oc)], writes=[("LOGW", oc)], tag=f"wCp{oc}")
                continue
            o("scalar", lambda e, oc=oc, ps=ps: e.activation(
                SCR[:, 1, :], ps[:, TT:2*TT], AF.Abs, bias=vcol("w_lora_bias", oc)),
              reads=[pk, ("VEC", VIX["w_lora_bias"]), ("SCR", 1)], writes=[("SCR", 1)],
              tag=f"wA{oc}")
            if ASTAGE == 33:
                o("scalar", lambda e, oc=oc: e.copy(LOGW[:, oc], SCR[:, 1, :]),
                  reads=[("SCR", 1), ("LOGW", oc)], writes=[("LOGW", oc)], tag=f"wCp{oc}")
                continue
            o("scalar", lambda e: e.activation(SCR[:, 2, :], SCR[:, 1, :], AF.Exp, scale=-1.0),
              reads=[("SCR", 1), ("SCR", 2)], writes=[("SCR", 2)], tag=f"wE{oc}")
            if ASTAGE == 34:
                o("scalar", lambda e, oc=oc: e.copy(LOGW[:, oc], SCR[:, 2, :]),
                  reads=[("SCR", 2), ("LOGW", oc)], writes=[("LOGW", oc)], tag=f"wCp{oc}")
                continue
            o("scalar", lambda e: e.activation(SCR[:, 3, :], SCR[:, 2, :], AF.Ln, bias=1.0),
              reads=[("SCR", 2), ("SCR", 3), "MISC"], writes=[("SCR", 3)], tag=f"wL{oc}")
            if ASTAGE == 345:
                o("scalar", lambda e, oc=oc: e.copy(LOGW[:, oc], SCR[:, 3, :]),
                  reads=[("SCR", 3), ("LOGW", oc)], writes=[("LOGW", oc)], tag=f"wCp{oc}")
                continue
            o("vector", lambda e, oc=oc, ps=ps: e.tensor_scalar(
                SCR[:, 4, :], ps[:, TT:2*TT], vcol("w_lora_bias", oc), 0.0, ALU.add, ALU.min),
              reads=[pk, ("VEC", VIX["w_lora_bias"]), ("SCR", 4), ("SCR", 1)],
              writes=[("SCR", 4)], tag=f"wM{oc}")
            if ASTAGE == 35:
                o("vector", lambda e, oc=oc: e.tensor_copy(LOGW[:, oc], SCR[:, 4, :]),
                  reads=[("SCR", 4), ("LOGW", oc)], writes=[("LOGW", oc)], tag=f"wCp{oc}")
                continue
            o("vector", lambda e, oc=oc: e.scalar_tensor_tensor(
                LOGW[:, oc], SCR[:, 4, :], -0.5, SCR[:, 3, :], ALU.add, ALU.subtract),
              reads=[("SCR", 4), ("SCR", 3), ("LOGW", oc)], writes=[("LOGW", oc)],
              tag=f"wF{oc}")

        if ASTAGE < 4:
            continue
        mb = emit_mix(2, "k")
        proj_1024(mb, Wk, "Wk", K_FM, "K_FM", extrakeys=["KPRK"])
        for cc in range(8):
            o("scalar", lambda e, cc=cc: e.activation(
                KKR[:, cc], K_FM[:, cc], AF.Identity, scale=vcol("k_k", cc)),
              reads=[("K_FM", cc), ("VEC", VIX["k_k"]), ("KKR", cc), "KPRK"],
              writes=[("KKR", cc), "KKRG"], tag=f"kr{cc}")
        mb = emit_mix(3, "v")
        proj_1024(mb, Wv, "Wv", V_FM, "V_FM")
        mb = emit_mix(4, "a")
        def al_mm(e, mb=mb):
            last = None
            for kc in range(8):
                last = e.matmul(PS2[0:64, 0:TT], ala[:, kc, :], MIX[:, mb, kc],
                                start=(kc == 0), stop=(kc == 7))
            return last
        o("tensor", al_mm, reads=["ala", "bPS2"] + [("MIX", mb, kc) for kc in range(8)],
          writes=["bPS2"], tag="alA")
        ZA1_ap = SCB[0:64, 5, :]
        o("scalar", lambda e: e.copy(ZA1_ap, PS2[0:64, 0:TT]),
          reads=["bPS2", ("SCR", 5)], writes=[("SCR", 5)], tag="alE")
        for oc in range(8):
            ps, pk = (PS2, "bPS2") if oc % 2 == 0 else (PS3, "bPS3")
            o("tensor", lambda e, oc=oc, ps=ps: e.matmul(
                ps[:, TT:2*TT], alb[:, oc*128:(oc+1)*128], ZA1_ap, start=True, stop=True),
              reads=["alb", ("SCR", 5), pk], writes=[pk], tag=f"alB{oc}")
            o("scalar", lambda e, oc=oc, ps=ps: e.activation(
                A_FM[:, oc], ps[:, TT:2*TT], AF.Sigmoid, bias=vcol("a_lora_bias", oc)),
              reads=[pk, ("VEC", VIX["a_lora_bias"]), ("A_FM", oc)],
              writes=[("A_FM", oc)], tag=f"aS{oc}")

        if ASTAGE < 5:
            continue
        KSQ_ap = SCB[:, 6, :]
        KIV_ap = SCB[0:2, 2, :]
        for cc in range(8):
            o("scalar", lambda e, cc=cc: e.activation(KSQ_ap, KKR[:, cc], AF.Square),
              reads=[("KKR", cc), ("SCR", 6), "KKRG"], writes=[("SCR", 6)], tag=f"kq{cc}")
            o("tensor", lambda e: e.matmul(PS6[0:2, 0:TT], ON2, KSQ_ap,
                                           start=True, stop=True),
              reads=[("SCR", 6), "ON2", "bPS6"], writes=["bPS6"], tag=f"kqm{cc}")
            o("scalar", lambda e: e.activation(SC2[0:2, 0, 0:TT], PS6[0:2, 0:TT], AF.Sqrt),
              reads=["bPS6", ("SC2", 0)], writes=[("SC2", 0)], tag=f"kqs{cc}")
            o("vector", lambda e: e.tensor_scalar_max(SC2[0:2, 1, 0:TT], SC2[0:2, 0, 0:TT], 1e-12),
              reads=[("SC2", 0), ("SC2", 1)], writes=[("SC2", 1)], tag=f"kqx{cc}")
            o("vector", lambda e: e.reciprocal(KIV_ap, SC2[0:2, 1, 0:TT]),
              reads=[("SC2", 1), ("SC2", 2)], writes=[("SC2", 2)], tag=f"kqi{cc}")
            o("tensor", lambda e: e.matmul(PS6[:, TT:2*TT], EX2, KIV_ap,
                                           start=True, stop=True),
              reads=[("SC2", 2), "EX2", "bPS6"], writes=["bPS6"], tag=f"kqb{cc}")
            o("vector", lambda e, cc=cc: e.tensor_tensor(
                KK_FM[:, cc], KKR[:, cc], PS6[:, TT:2*TT], ALU.mult),
              reads=["bPS6", ("KKR", cc), ("KK_FM", cc), "KKRG"],
              writes=[("KK_FM", cc)], tag=f"kqf{cc}")

        def gl_mm(e):
            last = None
            for kc in range(8):
                last = e.matmul(PS4[:, 0:TT], gla[:, kc, :], R_FM[:, kc],
                                start=(kc == 0), stop=(kc == 7))
            return last
        o("tensor", gl_mm, reads=["gla", "bPS4"] + [("R_FM", kc) for kc in range(8)],
          writes=["bPS4"], tag="glA")
        SGZ_ap = SCB[:, 7, :]
        o("scalar", lambda e: e.activation(SGZ_ap, PS4[:, 0:TT], AF.Sigmoid),
          reads=["bPS4", ("SCR", 7)], writes=[("SCR", 7)], tag="glS")
        for oc in range(8):
            ps, pk = (PS4, "bPS4") if oc % 2 == 0 else (PS5, "bPS5")
            o("tensor", lambda e, oc=oc, ps=ps: e.matmul(
                ps[:, TT:2*TT], glb[:, oc*128:(oc+1)*128], SGZ_ap, start=True, stop=True),
              reads=["glb", ("SCR", 7), pk], writes=[pk], tag=f"glB{oc}")
            o("scalar", lambda e, oc=oc, ps=ps: e.activation(
                G_FM[:, oc], ps[:, TT:2*TT], AF.Identity, bias=vcol("g_lora_bias", oc)),
              reads=[pk, ("VEC", VIX["g_lora_bias"]), ("KKR", oc)],
              writes=[("KKR", oc), ("G", oc), "KKRG"], tag=f"gF{oc}")

        K2S_ap = SCB[:, 3, :]
        for cc in range(8):
            o("vector", lambda e, cc=cc: e.tensor_scalar(
                K2S_ap, A_FM[:, cc], vcol("k_a", cc), NEG[:, cc, 5:6], ALU.mult, ALU.add),
              reads=[("A_FM", cc), ("VEC", VIX["k_a"]), ("NEG", 5), ("SCR", 6)],
              writes=[("SCR", 6)], tag=f"k2a{cc}")
            o("vector", lambda e, cc=cc: e.tensor_tensor(
                K2_FM[:, cc], K_FM[:, cc], K2S_ap, ALU.mult),
              reads=[("SCR", 6), ("K_FM", cc), ("K2_FM", cc), "KPRK"],
              writes=[("K2_FM", cc)], tag=f"k2b{cc}")

        if ASTAGE < 6:
            continue
        for cc in range(8):
            o("vector", lambda e, cc=cc: e.tensor_copy(CG[:, cc, 0:1], CG[:, cc, TT:TT+1]),
              reads=[("CG", cc)], writes=[("CG", cc)], tag=f"cc{cc}")
            o("vector", lambda e, cc=cc: e.tensor_tensor_scan(
                CG[:, cc, 1:TT+1], LOGW[:, cc], ON1.broadcast_to([128, TT]),
                CG[:, cc, 0:1], ALU.add, ALU.mult),
              reads=[("LOGW", cc), "ON1", ("CG", cc)], writes=[("CG", cc)],
              tag=f"cs{cc}")
            o("vector", lambda e, cc=cc: e.tensor_copy(CB[:, cc, :], CG[:, cc, 0:TT+1:LCH]),
              reads=[("CG", cc), ("CB", cc)], writes=[("CB", cc)], tag=f"cb{cc}")
            o("vector", lambda e, cc=cc: e.tensor_add(
                PMID[:, cc, :], CB[:, cc, 0:2], CB[:, cc, 1:3]),
              reads=[("CB", cc), ("PMID", cc)], writes=[("PMID", cc)], tag=f"p1{cc}")
            o("vector", lambda e, cc=cc: e.tensor_scalar_mul(PMID[:, cc, :], PMID[:, cc, :], 0.5),
              reads=[("PMID", cc)], writes=[("PMID", cc)], tag=f"p2{cc}")
            o("vector", lambda e, cc=cc: e.tensor_sub(ECL[:, cc, :], CB[:, cc, 1:3], CB[:, cc, 0:2]),
              reads=[("CB", cc), ("ECL", cc)], writes=[("ECL", cc)], tag=f"dc{cc}")
            o("scalar", lambda e, cc=cc: e.activation(EPHI[:, cc, :], ECL[:, cc, :], AF.Exp, scale=0.5),
              reads=[("ECL", cc), ("EPHI", cc)], writes=[("EPHI", cc)], tag=f"ep{cc}")
            o("scalar", lambda e, cc=cc: e.activation(ECL[:, cc, :], ECL[:, cc, :], AF.Exp),
              reads=[("ECL", cc), ("EPHI", cc)], writes=[("ECL", cc)], tag=f"ec{cc}")

        for cc in range(8):
            cg1 = CG[:, cc, 1:TT+1]
            cg0 = CG[:, cc, 0:TT]
            pmb = PMID[:, cc, :].unsqueeze(2).broadcast_to([128, 2, LCH])
            c0b = CB[:, cc, 0:2].unsqueeze(2).broadcast_to([128, 2, LCH])
            v3 = lambda ap: ap.rearrange("p (ch l) -> p ch l", l=LCH)
            ESC = SCR[:, 1, :]
            def esub(src3, bca, rkeys, tag_):
                o("vector", lambda e, src3=src3, bca=bca: e.tensor_tensor(
                    v3(ESC), src3, bca, ALU.subtract),
                  reads=rkeys + [("SCR", 1)], writes=[("SCR", 1)], tag=tag_)
            esub(v3(cg1), pmb, [("CG", cc), ("PMID", cc)], f"e1{cc}")
            o("scalar", lambda e, cc=cc: e.activation(FKB[:, cc], ESC, AF.Exp, scale=-1.0),
              reads=[("SCR", 1), ("FKB", cc)], writes=[("FKB", cc)], tag=f"fk{cc}")
            o("scalar", lambda e, cc=cc: e.activation(FRT[:, cc], ESC, AF.Exp),
              reads=[("SCR", 1), ("FRT", cc)], writes=[("FRT", cc)], tag=f"fr{cc}")
            o("vector", lambda e, cc=cc: e.tensor_tensor(
                SAR[:, cc, 1], R_FM[:, cc], FRT[:, cc], ALU.mult),
              reads=[("R_FM", cc), ("FRT", cc), ("SAR", cc, 1)], writes=[("SAR", cc, 1)],
              tag=f"pr{cc}")
            o("gpsimd", lambda e, cc=cc: e.tensor_tensor(
                SB_[:, cc], KK_FM[:, cc], A_FM[:, cc], ALU.mult),
              reads=[("KK_FM", cc), ("A_FM", cc), ("SB_", cc)], writes=[("SB_", cc)],
              tag=f"pb1{cc}")
            o("gpsimd", lambda e, cc=cc: e.tensor_tensor(
                SB_[:, cc], SB_[:, cc], FKB[:, cc], ALU.mult),
              reads=[("FKB", cc), ("SB_", cc)], writes=[("SB_", cc)], tag=f"pb2{cc}")
            o("gpsimd", lambda e, cc=cc: e.tensor_tensor(
                SK_[:, cc], K2_FM[:, cc], FKB[:, cc], ALU.mult),
              reads=[("K2_FM", cc), ("FKB", cc), ("SK_", cc)], writes=[("SK_", cc)],
              tag=f"pk{cc}")
            esub(v3(cg0), pmb, [("CG", cc), ("PMID", cc)], f"e3{cc}")
            o("scalar", lambda e, cc=cc: e.activation(FSH[:, cc], ESC, AF.Exp),
              reads=[("SCR", 1), ("FSH", cc)], writes=[("FSH", cc)], tag=f"fa{cc}")
            o("vector", lambda e, cc=cc: e.scalar_tensor_tensor(
                SAR[:, cc, 0], KK_FM[:, cc], -1.0, FSH[:, cc], ALU.mult, ALU.mult),
              reads=[("KK_FM", cc), ("FSH", cc), ("SAR", cc, 0)], writes=[("SAR", cc, 0)],
              tag=f"pa{cc}")
            esub(v3(cg0), c0b, [("CG", cc), ("CB", cc), ("SAR", cc, 0)], f"e4{cc}")
            o("scalar", lambda e, cc=cc: e.activation(FSH[:, cc], ESC, AF.Exp),
              reads=[("SCR", 1), ("FSH", cc), ("SAR", cc, 0)], writes=[("FSH", cc)],
              tag=f"fh{cc}")
            o("vector", lambda e, cc=cc: e.scalar_tensor_tensor(
                SAH[:, cc], KK_FM[:, cc], -1.0, FSH[:, cc], ALU.mult, ALU.mult),
              reads=[("KK_FM", cc), ("FSH", cc), ("SAH", cc)], writes=[("SAH", cc)],
              tag=f"ph{cc}")
            esub(v3(cg1), c0b, [("CG", cc), ("CB", cc), ("SAH", cc)], f"e5{cc}")
            o("scalar", lambda e, cc=cc: e.activation(FSH[:, cc], ESC, AF.Exp),
              reads=[("SCR", 1), ("FSH", cc), ("SAH", cc)], writes=[("FSH", cc)],
              tag=f"fh2{cc}")
            o("gpsimd", lambda e, cc=cc: e.tensor_tensor(
                SRH[:, cc], R_FM[:, cc], FSH[:, cc], ALU.mult),
              reads=[("R_FM", cc), ("FSH", cc), ("SRH", cc)], writes=[("SRH", cc)],
              tag=f"prh{cc}")

        if DBG:
            def tapfm(name, buf, keys):
                ap = dbg[name][:, t0:t0+TT].rearrange("(kc p) tt -> p kc tt", p=128)
                d("gpsimd", lambda e, ap=ap, buf=buf: e.dma_start(out=ap, in_=buf),
                  reads=keys, tag=f"tp{name}")
            tapfm("d_hs", XH_FM[:, :, 1:TT+1], [("XH_FM",)])
            tapfm("d_r", R_FM, [("R_FM", i) for i in range(8)])
            tapfm("d_logw", LOGW, [("LOGW", i) for i in range(8)])
            tapfm("d_k2", K2_FM, [("K2_FM", i) for i in range(8)])
            tapfm("d_a", A_FM, [("A_FM", i) for i in range(8)])
            tapfm("d_kk", KK_FM, [("KK_FM", i) for i in range(8)])
            tapfm("d_g", G_FM, [("G", i) for i in range(8)] + ["KKRG"])
            tapfm("d_v", V_FM, [("V_FM", i) for i in range(8)])

        if BUILD_STAGE < 2:
            continue

        def scan_head_ops(h):
            ops = []
            cc = h // 2
            bh = (h % 2) * 64
            rot = h % 2
            zrot = (h % 2) * 2 + (t % 2)
            I = (t, h)
            a_sl = SAR[bh:bh+64, cc, 0, :]
            r_sl = SAR[bh:bh+64, cc, 1, :]
            ar_sl = SAR[bh:bh+64, cc, 0:2, :]
            b_sl = SB_[bh:bh+64, cc, :]
            k_sl = SK_[bh:bh+64, cc, :]
            v_sl = V_FM[bh:bh+64, cc, :]
            ah_sl = SAH[bh:bh+64, cc, :]
            rh_sl = SRH[bh:bh+64, cc, :]
            pkeys = [("SAR", cc, 0), ("SAR", cc, 1), ("SB_", cc), ("SK_", cc),
                     ("V_FM", cc), ("SAH", cc), ("SRH", cc)]
            BKA = (PS0, "bPS0") if rot == 0 else (PS3, "bPS3")
            BKB = (PS1, "bPS1") if rot == 0 else (PS4, "bPS4")
            BKC = (PS2, "bPS2") if rot == 0 else (PS5, "bPS5")
            g3b = (PS6, "bPS6")
            g1_r = BKA[0][:, 0:128]; g2_r = BKA[0][:, 128:384]
            W_rr = BKA[0][0:64, 384:448]; Yl_rr = BKA[0][0:64, 448:512]
            g3_r = g3b[0][:, rot * 256:(rot + 1) * 256]
            csv_r = BKB[0][:, 0:64]; apl_r = BKB[0][:, 64:192]
            sqt_r = BKB[0][:, 192:320]; sqp_r = BKB[0][:, 320:448]
            st_r = BKB[0][0:64, 448:512]
            zt_r = BKC[0][0:64, 0:128]; o_r = BKC[0][0:64, 128:256]
            GAk = BKA[1]; GBk = BKB[1]
            kg1 = ("psG1", rot); kg2 = ("psG2", rot); kg3 = ("psG3", rot)
            kapl = ("psAPL", rot); kcsv = ("psCSV", rot); kst = ("psST", rot)
            ksqt = ("psSQT", rot); ksqp = ("psSQP", rot)
            kzt = ("psZT", rot); ko = ("psO", rot)
            kW = {0: ("psW", rot), 1: ("psW", rot)}
            kYl = {0: ("psYl", rot), 1: ("psYl", rot)}
            W_r = {0: W_rr, 1: W_rr}
            Yl_r = {0: Yl_rr, 1: Yl_rr}
            bguards = ["bPS0", "bPS1", "bPS2", "bPS3", "bPS4", "bPS5", "bPS6"]

            def O(eng, fn, reads=(), writes=(), tag=""):
                ops.append((eng, fn, list(reads), list(writes), tag))

            def tps(e, ah_sl=ah_sl, b_sl=b_sl, k_sl=k_sl, v_sl=v_sl, bh=bh):
                idn = IDB[bh:bh+64, bh:bh+64]
                e.transpose(PSB[:, 0:64], b_sl, idn)
                e.transpose(PSB[:, 64:128], k_sl, idn)
                e.transpose(PSB[:, 128:192], v_sl, idn)
                return e.transpose(PSB[:, 192:256], ah_sl, idn)
            O("tensor", tps, reads=pkeys + ["IDB", "bPSB"], writes=["bPSB"], tag=f"sT{I}")
            O("scalar", lambda e, rot=rot: e.copy(TMX[:, rot, :], PSB[:, 0:192]),
              reads=["bPSB", ("TMX", rot)], writes=[("TMX", rot)], tag=f"sTe{I}")
            O("scalar", lambda e, rot=rot: e.copy(YST[:, rot, 0, 0:64], PSB[:, 192:256]),
              reads=["bPSB", ("YST", rot, 0)], writes=[("YST", rot, 0)], tag=f"sTf{I}")

            def grams(e, a_sl=a_sl, b_sl=b_sl, k_sl=k_sl, ar_sl=ar_sl, g1_r=g1_r, g2_r=g2_r, g3_r=g3_r):
                e.matmul(g1_r, a_sl, b_sl, start=True, stop=True)
                e.matmul(g2_r, b_sl, ar_sl, start=True, stop=True)
                return e.matmul(g3_r, k_sl, ar_sl, start=True, stop=True)
            O("tensor", grams, reads=pkeys + [GAk, g3b[1], kg1, kg2, kg3],
              writes=[kg1, kg2, kg3], tag=f"sG{I}")
            O("vector", lambda e, rot=rot, g1_r=g1_r: e.select(MPL[:, rot, 0], maskA, g1_r, Z128, add_drain=True),
              reads=[kg1, GAk, "MSK", "ZRO", ("MPL", rot, 0)], writes=[("MPL", rot, 0)], tag=f"mA{I}")
            O("vector", lambda e, rot=rot, g2_r=g2_r: e.select(MTR[:, rot, 0], maskB, g2_r[:, 0:128], Z128, add_drain=True),
              reads=[kg2, GAk, "MSK", "ZRO", ("MTR", rot, 0)], writes=[("MTR", rot, 0)], tag=f"mB{I}")
            O("vector", lambda e, rot=rot, g2_r=g2_r: e.select(BRU[:, rot, :], maskC, g2_r[:, 128:256], Z128, add_drain=True),
              reads=[kg2, GAk, "MSK", "ZRO", ("BRU", rot)], writes=[("BRU", rot)], tag=f"mR{I}")
            O("vector", lambda e, rot=rot, g3_r=g3_r: e.select(CTM[:, rot, :], maskB, g3_r[:, 0:128], Z128, add_drain=True),
              reads=[kg3, g3b[1], "MSK", "ZRO", ("CTM", rot)], writes=[("CTM", rot)], tag=f"mC{I}")
            O("vector", lambda e, rot=rot, g3_r=g3_r: e.select(KRU[:, rot, :], maskC, g3_r[:, 128:256], Z128, add_drain=True),
              reads=[kg3, g3b[1], "MSK", "ZRO", ("KRU", rot)], writes=[("KRU", rot)], tag=f"mK{I}")

            O("tensor", lambda e, rot=rot, csv_r=csv_r: e.matmul(
                csv_r, CTM[:, rot, :], TMX[:, rot, 128:192], start=True, stop=True),
              reads=[("CTM", rot), ("TMX", rot), GBk, kcsv], writes=[kcsv], tag=f"sC{I}")
            O("scalar", lambda e, rot=rot, csv_r=csv_r: e.copy(YST[:, rot, 0, 64:128], csv_r),
              reads=[kcsv, GBk, ("YST", rot, 0)], writes=[("YST", rot, 0)], tag=f"sCe{I}")

            ycur = 0
            mcur = 0
            for lvl in range(6):
                O("tensor", lambda e, rot=rot, mcur=mcur, ycur=ycur, apl_r=apl_r: e.matmul(
                    apl_r, MTR[:, rot, mcur], YST[:, rot, ycur, :], start=True, stop=True),
                  reads=[("MTR", rot, mcur), ("YST", rot, ycur), GBk, kapl],
                  writes=[kapl], tag=f"ap{lvl}{I}")
                O("vector", lambda e, rot=rot, ycur=ycur, apl_r=apl_r: e.tensor_add(
                    YST[:, rot, 1 - ycur, :], apl_r, YST[:, rot, ycur, :]),
                  reads=[kapl, GBk, ("YST", rot, ycur), ("YST", rot, 1 - ycur)],
                  writes=[("YST", rot, 1 - ycur)], tag=f"ae{lvl}{I}")
                ycur = 1 - ycur
                if lvl < 5:
                    def sq(e, rot=rot, mcur=mcur, sqt_r=sqt_r, sqp_r=sqp_r):
                        e.matmul(sqt_r, MPL[:, rot, mcur], MTR[:, rot, mcur], start=True, stop=True)
                        return e.matmul(sqp_r, MTR[:, rot, mcur], MPL[:, rot, mcur], start=True, stop=True)
                    O("tensor", sq, reads=[("MPL", rot, mcur), ("MTR", rot, mcur), GBk, ksqt, ksqp],
                      writes=[ksqt, ksqp], tag=f"sq{lvl}{I}")
                    O("scalar", lambda e, rot=rot, mcur=mcur, sqt_r=sqt_r: e.copy(
                        MTR[:, rot, 1 - mcur], sqt_r),
                      reads=[ksqt, GBk, ("MTR", rot, 1 - mcur)], writes=[("MTR", rot, 1 - mcur)],
                      tag=f"sa{lvl}{I}")
                    O("scalar", lambda e, rot=rot, mcur=mcur, sqp_r=sqp_r: e.copy(
                        MPL[:, rot, 1 - mcur], sqp_r),
                      reads=[ksqp, GBk, ("MPL", rot, 1 - mcur)], writes=[("MPL", rot, 1 - mcur)],
                      tag=f"sb{lvl}{I}")
                    mcur = 1 - mcur
            ykey = ("YST", rot, ycur)

            O("tensor", lambda e, rot=rot, ycur=ycur, zt_r=zt_r: e.matmul(
                zt_r, YST[:, rot, ycur, 0:64], BRU[:, rot, :], start=True, stop=True),
              reads=[ykey, ("BRU", rot), BKC[1], kzt], writes=[kzt], tag=f"zt{I}")
            O("vector", lambda e, zrot=zrot, rh_sl=rh_sl, bh=bh, zt_r=zt_r: e.tensor_add(
                ZTP[bh:bh+64, zrot, :], zt_r, rh_sl),
              reads=[kzt, BKC[1], ("SRH", cc), ("ZTP", zrot)], writes=[("ZTP", zrot)], tag=f"ze{I}")

            for ch in range(2):
                cb0 = ch * 64
                chk = (t, h, ch)
                wbank = BKA[1]
                O("tensor", lambda e, rot=rot, cb0=cb0, ycur=ycur, ch=ch: e.matmul(
                    W_r[ch], YST[cb0:cb0+64, rot, ycur, 0:64],
                    TMX[cb0:cb0+64, rot, 0:64], start=True, stop=True),
                  reads=[ykey, ("TMX", rot), wbank, kW[ch]], writes=[kW[ch]], tag=f"w{chk}")
                O("scalar", lambda e, zrot=zrot, bh=bh, ch=ch: e.copy(
                    WPD[bh:bh+64, zrot, ch, :], W_r[ch]),
                  reads=[kW[ch], BKA[1], ("WPD", zrot, ch)], writes=[("WPD", zrot, ch)], tag=f"we{chk}")
                ybank = BKA[1]
                def yloc(e, rot=rot, cb0=cb0, ycur=ycur, ch=ch):
                    e.matmul(Yl_r[ch], TMX[cb0:cb0+64, rot, 0:64],
                             YST[cb0:cb0+64, rot, ycur, 64:128], start=True, stop=False)
                    return e.matmul(Yl_r[ch], TMX[cb0:cb0+64, rot, 64:128],
                                    TMX[cb0:cb0+64, rot, 128:192], start=False, stop=True)
                O("tensor", yloc, reads=[ykey, ("TMX", rot), ybank, kYl[ch]],
                  writes=[kYl[ch]], tag=f"y{chk}")
                ephi_col = EPHI[bh:bh+64, cc, ch:ch+1]
                O("vector", lambda e, ch=ch, ephi_col=ephi_col, bh=bh: e.tensor_tensor(
                    YLC[bh:bh+64, ch, :], Yl_r[ch], ephi_col.broadcast_to([64, 64]), ALU.mult),
                  reads=[kYl[ch], BKA[1], ("EPHI", cc), ("YLC", rot, ch)],
                  writes=[("YLC", rot, ch)], tag=f"ye{chk}")

            for ch in range(2):
                gch = gchunk0 + ch
                sbuf = gch % 2
                chk = (t, h, ch)
                statekey = ("SST", h, sbuf)
                nstatekey = ("SST", h, 1 - sbuf)
                ecl_col = ECL[bh:bh+64, cc, ch:ch+1]
                ephi_col = EPHI[bh:bh+64, cc, ch:ch+1]

                def o_mms(e, rot=rot, zrot=zrot, ch=ch, cc=cc, h=h, sbuf=sbuf, ycur=ycur, o_r=o_r):
                    orr = o_r[:, ch*64:(ch+1)*64]
                    e.matmul(orr, BRU[:, rot, ch*64:(ch+1)*64],
                             YST[:, rot, ycur, 64:128], start=True, stop=False)
                    e.matmul(orr, KRU[:, rot, ch*64:(ch+1)*64],
                             TMX[:, rot, 128:192], start=False, stop=False)
                    return e.matmul(orr, ZTP[:, zrot, ch*64:(ch+1)*64],
                                    SST[:, h, sbuf, :], start=False, stop=True)
                O("tensor", o_mms,
                  reads=[ykey, ("BRU", rot), ("KRU", rot), ("TMX", rot), ("ZTP", zrot),
                         statekey, BKC[1], ko],
                  writes=[ko], tag=f"om{chk}")
                O("scalar", lambda e, h=h, ch=ch, o_r=o_r: e.copy(
                    OTM[ch*64:(ch+1)*64, h*64:(h+1)*64], o_r[:, ch*64:(ch+1)*64]),
                  reads=[ko, BKC[1], ("OTM", h), "OX2"], writes=[("OTM", h), "OX2"], tag=f"oe{chk}")

                def st_mm(e, zrot=zrot, h=h, sbuf=sbuf, bh=bh, st_r=st_r, ch=ch):
                    e.matmul(st_r, WPD[:, zrot, ch, :], SST[:, h, sbuf, :], start=True, stop=False)
                    return e.matmul(st_r, IDB[:, bh:bh+64], SST[:, h, sbuf, :],
                                    start=False, stop=True)
                O("tensor", st_mm, reads=[("WPD", zrot, ch), statekey, "IDB", GBk, kst],
                  writes=[kst], tag=f"st{chk}")
                O("vector", lambda e, h=h, sbuf=sbuf, bh=bh, ecl_col=ecl_col: e.tensor_tensor(
                    STMP[bh:bh+64, :], SST[bh:bh+64, h, sbuf, :],
                    ecl_col.broadcast_to([64, 64]), ALU.mult),
                  reads=[statekey, ("ECL", cc), ("STMP", rot)],
                  writes=[("STMP", rot)], tag=f"s1{chk}")
                O("vector", lambda e, bh=bh, ch=ch: e.tensor_add(
                    STMP[bh:bh+64, :], STMP[bh:bh+64, :], YLC[bh:bh+64, ch, :]),
                  reads=[("YLC", rot, ch), ("STMP", rot)],
                  writes=[("STMP", rot)], tag=f"s1b{chk}")
                O("vector", lambda e, h=h, sbuf=sbuf, bh=bh, ephi_col=ephi_col, st_r=st_r: e.tensor_tensor(
                    SST[bh:bh+64, h, 1 - sbuf, :], st_r,
                    ephi_col.broadcast_to([64, 64]), ALU.mult),
                  reads=[kst, GBk, ("EPHI", cc), nstatekey],
                  writes=[nstatekey], tag=f"s2{chk}")
                O("vector", lambda e, h=h, sbuf=sbuf, bh=bh: e.tensor_add(
                    SST[bh:bh+64, h, 1 - sbuf, :], SST[bh:bh+64, h, 1 - sbuf, :],
                    STMP[bh:bh+64, :]),
                  reads=[("STMP", rot), nstatekey],
                  writes=[nstatekey], tag=f"s2b{chk}")
            return ops

        from itertools import zip_longest
        BG = {"bPS0", "bPS1", "bPS2", "bPS3", "bPS4", "bPS5", "bPS6"}
        for hp in range(NH // 2):
            ops0 = scan_head_ops(2 * hp)
            ops1 = scan_head_ops(2 * hp + 1)
            g0 = [ops0[0:3]] + [[x] for x in ops0[3:]]
            g1 = [ops1[0:3]] + [[x] for x in ops1[3:]]
            import os as _os
            seq = []
            if _os.environ.get("SEQSCAN"):
                for grp in g0 + g1:
                    seq.extend(grp)
            else:
                for pair in zip_longest(g0, g1):
                    for grp in pair:
                        if grp:
                            seq.extend(grp)
            for eng, fn, reads, writes, tag in seq:
                newreads = [k for k in reads if k not in BG]
                newwrites = list(writes) + [k for k in reads if k in BG]
                o(eng, fn, reads=newreads, writes=newwrites, tag=tag)

        if DBG:
            ap = dbg["d_o"][t0:t0+TT, :]
            d("gpsimd", lambda e, ap=ap: e.dma_start(out=ap, in_=OTM),
              reads=[("OTM", h) for h in range(NH)] + ["OX2"], tag="tpo")

        if BUILD_STAGE < 3:
            continue

        otkeys = [("OTM", h) for h in range(NH)] + ["OX2"]
        o3v = OTM.rearrange("p (s dd) -> p s dd", dd=HD)
        o("vector", lambda e, o3v=o3v: e.reduce_sum(MU, o3v, AX.X),
          reads=otkeys + ["MUb"], writes=["MUb"], tag="gm")
        o("scalar", lambda e: e.activation(DUMP, OTM, AF.Square),
          reads=otkeys + ["DUMP"], writes=["DUMP"], tag="gq")
        o("vector", lambda e: e.reduce_sum(
            SSQ, DUMP.rearrange("p (s dd) -> p s dd", dd=HD), AX.X),
          reads=["DUMP", "SSQb"], writes=["SSQb"], tag="gq2")
        o("vector", lambda e: e.tensor_scalar_mul(MU, MU, 1.0 / HD),
          reads=["MUb"], writes=["MUb"], tag="gm2")
        o("vector", lambda e: e.tensor_tensor(MRS, MU, MU, ALU.mult),
          reads=["MUb", "MRSb"], writes=["MRSb"], tag="gm3")
        o("vector", lambda e: e.scalar_tensor_tensor(
            SSQ, SSQ, 1.0 / HD, MRS, ALU.mult, ALU.subtract),
          reads=["SSQb", "MRSb"], writes=["SSQb"], tag="gv")
        o("scalar", lambda e: e.activation(SSQ, SSQ, AF.Sqrt, bias=1e-5),
          reads=["SSQb", "MISC"], writes=["SSQb"], tag="gs")
        o("vector", lambda e: e.reciprocal(RSTD, SSQ),
          reads=["SSQb", "RSTDb"], writes=["RSTDb"], tag="gr")
        o("vector", lambda e: e.tensor_tensor(MRS, MU, RSTD, ALU.mult),
          reads=["MUb", "RSTDb", "MRSb"], writes=["MRSb"], tag="gmr")
        o("vector", lambda e, o3v=o3v: e.tensor_tensor(
            ON_TM.rearrange("p (s dd) -> p s dd", dd=HD), o3v,
            RSTD.unsqueeze(2).broadcast_to([128, NH, HD]), ALU.mult),
          reads=otkeys + ["RSTDb", "XHC"], writes=["XHC"], tag="ga")
        o("vector", lambda e: e.tensor_tensor(
            ON_TM.rearrange("p (s dd) -> p s dd", dd=HD),
            ON_TM.rearrange("p (s dd) -> p s dd", dd=HD),
            MRS.unsqueeze(2).broadcast_to([128, NH, HD]), ALU.subtract),
          reads=["XHC", "MRSb"], writes=["XHC"], tag="gb")
        def on_t(e):
            last = None
            for cc in range(8):
                last = e.transpose(PSB[:, cc*128:(cc+1)*128],
                                   ON_TM[:, cc*128:(cc+1)*128], IDB)
            return last
        o("tensor", on_t, reads=["XHC", "IDB", "bPSB"], writes=["bPSB"], tag="oT")
        o("scalar", lambda e: e.copy(ON_FM, PSB.rearrange("p (cc n) -> p cc n", n=TT)),
          reads=["bPSB", "ON_FMb"], writes=["ON_FMb"], tag="oTe")
        for cc in range(8):
            o("vector", lambda e, cc=cc: e.tensor_tensor(
                PRK[:, cc], R_FM[:, cc], K2_FM[:, cc], ALU.mult),
              reads=[("R_FM", cc), ("K2_FM", cc), ("K_FM", cc), "KPRK"],
              writes=[("K_FM", cc), "KPRK"], tag=f"bp{cc}")
            o("tensor", lambda e, cc=cc: e.matmul(
                PS0[0:2, 0:TT], rk2[:, cc, :], PRK[:, cc], start=True, stop=True),
              reads=[("K_FM", cc), ("rk2", cc), "bPS0"], writes=["bPS0"], tag=f"bs{cc}")
            o("scalar", lambda e, cc=cc: e.copy(SBC[:, cc, :], PS0[0:2, 0:TT]),
              reads=["bPS0", ("SBC", cc)], writes=[("SBC", cc)], tag=f"be{cc}")
            o("tensor", lambda e, cc=cc: e.matmul(
                PS1[:, 0:TT], EX2, SBC[:, cc, :], start=True, stop=True),
              reads=[("SBC", cc), "EX2", "bPS1"], writes=["bPS1"], tag=f"bb{cc}")
            o("vector", lambda e, cc=cc: e.tensor_tensor(
                OG[:, cc], PS1[:, 0:TT], V_FM[:, cc], ALU.mult),
              reads=["bPS1", ("V_FM", cc), ("OG", cc)], writes=[("OG", cc)], tag=f"bv{cc}")
            o("vector", lambda e, cc=cc: e.tensor_add(OG[:, cc], OG[:, cc], ON_FM[:, cc, :]),
              reads=[("OG", cc), "ON_FMb"], writes=[("OG", cc)], tag=f"bo{cc}")
            o("vector", lambda e, cc=cc: e.tensor_scalar(
                OG[:, cc], OG[:, cc], vcol("gn_weight", cc), vcol("gn_bias", cc),
                ALU.mult, ALU.add),
              reads=[("OG", cc), ("VEC", VIX["gn_weight"]), ("VEC", VIX["gn_bias"])],
              writes=[("OG", cc)], tag=f"bw{cc}")
            o("vector", lambda e, cc=cc: e.tensor_tensor(OG[:, cc], OG[:, cc], G_FM[:, cc], ALU.mult),
              reads=[("OG", cc), ("G", cc), "KKRG"], writes=[("OG", cc)], tag=f"bg{cc}")

        if DBG:
            ap = dbg["d_of"][:, t0:t0+TT].rearrange("(kc p) tt -> p kc tt", p=128)
            d("gpsimd", lambda e, ap=ap: e.dma_start(out=ap, in_=OG),
              reads=[("OG", cc) for cc in range(8)], tag="tpof")

        def womm(e):
            last = None
            for kc in range(8):
                last = e.matmul(PS0[:, 0:512], OG[:, kc, :], Wo[:, kc, 0:512],
                                start=(kc == 0), stop=(kc == 7))
            for kc in range(8):
                last = e.matmul(PS1[:, 0:512], OG[:, kc, :], Wo[:, kc, 512:1024],
                                start=(kc == 0), stop=(kc == 7))
            return last
        o("tensor", womm,
          reads=[("OG", cc) for cc in range(8)] + ["Wo", "bPS0", "bPS1"],
          writes=["bPS0", "bPS1"], tag="wo")
        o("vector", lambda e, tb=tb: e.tensor_add(
            X2T[:, 0:512], PS0[:, 0:512], XT[:, tb, 0:512]),
          reads=["bPS0", ("XT", tb)], writes=["X2Ta", "OX2"], tag="x2a")
        o("vector", lambda e, tb=tb: e.tensor_add(
            X2T[:, 512:1024], PS1[:, 0:512], XT[:, tb, 512:1024]),
          reads=["bPS1", ("XT", tb)], writes=["X2Tb", "OX2"], tag="x2b")
        d("sync", lambda e, t0=t0: e.dma_start(out=x2s[t0:t0+TT, :], in_=X2T),
          reads=["X2Ta", "X2Tb", "OX2"], writes=[("x2sw", t)], tag=f"x2w{t}")
        kb.last_w[("x2s", t // 2)] = kb.last_w[("x2sw", t)]
        emit_ln(X2T, ["X2Ta", "X2Tb", "OX2"], Y_TM, "XHC", f"ln2_{t}")
        def y_t(e):
            last = None
            for cc in range(8):
                last = e.transpose(PSB[:, cc*128:(cc+1)*128],
                                   Y_TM[:, cc*128:(cc+1)*128], IDB)
            return last
        o("tensor", y_t, reads=["XHC", "IDB", "bPSB"], writes=["bPSB"], tag="yT")
        o("scalar", lambda e: e.copy(Y_FM, PSB.rearrange("p (cc n) -> p cc n", n=TT)),
          reads=["bPSB", "Y_FMb"], writes=["Y_FMb"], tag="yTe")
        d("sync", lambda e, t=t: e.dma_start(
            out=ys[t].rearrange("(kc p) tt -> p kc tt", p=128), in_=Y_FM),
          reads=["Y_FMb"], writes=[("ys", t)], tag=f"yw{t}")

        def rmm(e):
            last = None
            for kc in range(8):
                last = e.matmul(PS2[:, 384:392], Y_FM[:, kc, :], gwT[:, kc, :],
                                start=(kc == 0), stop=(kc == 7))
            return last
        o("tensor", rmm, reads=["Y_FMb", "gwT", "bPS2"], writes=["bPS2"], tag="rt")
        o("vector", lambda e: e.tensor_copy(RT, PS2[:, 384:392]),
          reads=["bPS2", "RTb"], writes=["RTb"], tag="rte")
        tp = TOP
        A_, B_, Cc, Dd = tp[:, 2:10], tp[:, 10:18], tp[:, 18:26], tp[:, 26:34]
        o("vector", lambda e: e.reduce_max(tp[:, 0:1], RT, AX.X),
          reads=["RTb", "T0"], writes=["T0"], tag="t0")
        o("vector", lambda e: e.tensor_single_scalar(A_, RT, tp[:, 0:1], ALU.is_ge),
          reads=["RTb", "T0", "TA"], writes=["TA"], tag="t1")
        o("vector", lambda e: e.tensor_tensor_scan(
            B_, A_, ON1.broadcast_to([128, 8]), 0.0, ALU.add, ALU.mult),
          reads=["TA", "ON1", "TBk"], writes=["TBk"], tag="t2")
        o("vector", lambda e: e.tensor_single_scalar(Cc, B_, 1.0, ALU.is_equal),
          reads=["TBk", "TC"], writes=["TC"], tag="t3")
        o("vector", lambda e: e.tensor_tensor(B_, A_, Cc, ALU.mult),
          reads=["TA", "TC", "TBk"], writes=["TBk"], tag="t4")
        o("vector", lambda e: e.scalar_tensor_tensor(Cc, B_, -1e30, RT, ALU.mult, ALU.add),
          reads=["TBk", "RTb", "TC"], writes=["TC"], tag="t5")
        o("vector", lambda e: e.reduce_max(tp[:, 1:2], Cc, AX.X),
          reads=["TC", "T1"], writes=["T1"], tag="t6")
        o("vector", lambda e: e.tensor_single_scalar(A_, Cc, tp[:, 1:2], ALU.is_ge),
          reads=["TC", "T1", "TA"], writes=["TA"], tag="t7")
        o("vector", lambda e: e.tensor_tensor_scan(
            Dd, A_, ON1.broadcast_to([128, 8]), 0.0, ALU.add, ALU.mult),
          reads=["TA", "ON1", "TD"], writes=["TD"], tag="t8")
        o("vector", lambda e: e.tensor_single_scalar(Cc, Dd, 1.0, ALU.is_equal),
          reads=["TD", "TC"], writes=["TC"], tag="t9")
        o("vector", lambda e: e.tensor_tensor(Dd, A_, Cc, ALU.mult),
          reads=["TA", "TC", "TD"], writes=["TD"], tag="ta")
        o("vector", lambda e: e.tensor_sub(tp[:, 34:35], tp[:, 0:1], tp[:, 1:2]),
          reads=["T0", "T1", "Tdm"], writes=["Tdm"], tag="tb")
        o("scalar", lambda e: e.activation(tp[:, 35:36], tp[:, 34:35], AF.Sigmoid),
          reads=["Tdm", "Tw1"], writes=["Tw1"], tag="tc")
        o("vector", lambda e: e.tensor_scalar(
            tp[:, 36:37], tp[:, 35:36], -1.0, 1.0, ALU.mult, ALU.add),
          reads=["Tw1", "Tw2"], writes=["Tw2"], tag="td")
        o("vector", lambda e: e.tensor_scalar_mul(WFT, B_, tp[:, 35:36]),
          reads=["TBk", "Tw1", "WFTb"], writes=["WFTb"], tag="te")
        o("vector", lambda e: e.scalar_tensor_tensor(
            WFT, Dd, tp[:, 36:37], WFT, ALU.mult, ALU.add),
          reads=["TD", "Tw2", "WFTb"], writes=["WFTb"], tag="tf")
        o("tensor", lambda e: e.transpose(PS2[0:8, 384:512], WFT, IDF),
          reads=["WFTb", "IDF", "bPS2"], writes=["bPS2"], tag="tg")
        o("vector", lambda e: e.tensor_copy(WF8, PS2[0:8, 384:512]),
          reads=["bPS2", "WF8b"], writes=["WF8b"], tag="th")
        d("sync", lambda e, t0=t0: e.dma_start(out=ws[:, t0:t0+TT], in_=WF8),
          reads=["WF8b"], writes=[("ws", t)], tag=f"ww{t}")

        if DBG:
            ap = dbg["d_x2"][t0:t0+TT, :]
            d("gpsimd", lambda e, ap=ap: e.dma_start(out=ap, in_=X2T),
              reads=["X2Ta", "X2Tb", "OX2"], tag="tpx2")
            ap2 = dbg["d_y"][:, t0:t0+TT].rearrange("(kc p) tt -> p kc tt", p=128)
            d("gpsimd", lambda e, ap2=ap2: e.dma_start(out=ap2, in_=Y_FM),
              reads=["Y_FMb"], tag="tpy")
            ap3 = dbg["d_wf"][:, t0:t0+TT]
            d("gpsimd", lambda e, ap3=ap3: e.dma_start(out=ap3, in_=WF8),
              reads=["WF8b"], tag="tpwf")

    if BUILD_STAGE >= 3:
        kb.fence()
        d("gpsimd", lambda e: e.dma_start(out=W8, in_=ws),
          reads=[("ws", t) for t in range(NT)], writes=["W8"], tag="w8")
        for ei in range(NE):
            d("gpsimd", lambda e, ei=ei: e.dma_start(out=w1bf, in_=W1_in[ei]),
              reads=["w1bf"], writes=["w1bf"], tag=f"w1c{ei}")
            d("gpsimd", lambda e, ei=ei: e.dma_start(out=w2bf, in_=W2_in[ei]),
              reads=["w2bf"], writes=["w2bf"], tag=f"w2c{ei}")
            for cc in range(8):
                d("sync", lambda e, cc=cc: e.dma_start_transpose(
                    out=W1T[:, cc, :], in_=w1bf[:, cc*128:(cc+1)*128]),
                  reads=["w1bf", ("W1T", cc)], writes=[("W1T", cc)], tag=f"w1T{ei}_{cc}")
            for hc in range(32):
                d("sync", lambda e, hc=hc: e.dma_start_transpose(
                    out=W2T[:, hc, :], in_=w2bf[:, hc*128:(hc+1)*128]),
                  reads=["w2bf", ("W2T", hc)], writes=[("W2T", hc)], tag=f"w2T{ei}_{hc}")
            d("sync", lambda e, ei=ei: e.dma_start(out=B1C, in_=b1_in[ei]),
              reads=["B1C"], writes=["B1C"], tag=f"b1{ei}")
            d("sync", lambda e, ei=ei: e.dma_start(out=B2C, in_=b2_in[ei]),
              reads=["B2C"], writes=["B2C"], tag=f"b2{ei}")
            for bt in range(NTB):
                tb = bt % 2
                bt0 = bt * TB
                for bb in range(2):
                    d("sync", lambda e, bt=bt, tb=tb, bb=bb: e.dma_start(
                        out=YTL[:, tb, :, bb*TT:(bb+1)*TT],
                        in_=ys[2*bt+bb].rearrange("(kc p) n -> p kc n", p=128)),
                      reads=[("ys", 2*bt+bb), ("YTL", tb)],
                      writes=[("YTL", tb)], tag=f"ylo{ei}_{bt}_{bb}")
                o("tensor", lambda e, ei=ei, bt0=bt0: e.matmul(
                    PS2[:, TB:2*TB], E8S[:, ei*128:(ei+1)*128],
                    W8[:, bt0:bt0+TB], start=True, stop=True),
                  reads=["W8", "E8S", "bPS2"], writes=["bPS2"], tag=f"wb{ei}_{bt}")
                o("vector", lambda e: e.tensor_copy(WB, PS2[:, TB:2*TB]),
                  reads=["bPS2", "WBb"], writes=["WBb"], tag=f"we{ei}_{bt}")
                for hc in range(32):
                    ps, pk = (PS0, "bPS0") if hc % 2 == 0 else (PS1, "bPS1")
                    def mm1(e, hc=hc, tb=tb, ps=ps):
                        last = None
                        for kc in range(8):
                            last = e.matmul(ps[:, 0:TB], W1T[:, kc, hc*128:(hc+1)*128],
                                            YTL[:, tb, kc], start=(kc == 0), stop=(kc == 7))
                        return last
                    o("tensor", mm1,
                      reads=[("W1T", kc) for kc in range(8)] + [("YTL", tb), pk],
                      writes=[pk], tag=f"m1_{ei}_{bt}_{hc}")
                    o("scalar", lambda e, hc=hc, ps=ps: e.activation(
                        HT[:, hc, :], ps[:, 0:TB], AF.Relu, bias=B1C[:, hc:hc+1]),
                      reads=[pk, "B1C", ("HT", hc)], writes=[("HT", hc)],
                      tag=f"rl_{ei}_{bt}_{hc}")
                for oc in range(8):
                    ps, pk = (PS4, "bPS4") if oc % 2 == 0 else (PS5, "bPS5")
                    def mm2(e, oc=oc, ps=ps):
                        last = None
                        for hc in range(32):
                            last = e.matmul(ps[:, 0:TB], W2T[:, hc, oc*128:(oc+1)*128],
                                            HT[:, hc, :], start=(hc == 0), stop=(hc == 31))
                        return last
                    o("tensor", mm2,
                      reads=[("W2T", hc) for hc in range(32)] + [("HT", hc) for hc in range(32)] + [pk],
                      writes=[pk], tag=f"m2_{ei}_{bt}_{oc}")
                    if ei == NE - 1:
                        d("sync", lambda e, bt=bt, oc=oc, tb=tb: e.dma_start(
                            out=ACC1[:, tb], in_=accs[bt, oc*128:(oc+1)*128, :]),
                          reads=[("accs", bt), ("ACC1", tb)], writes=[("ACC1", tb)],
                          tag=f"a1_{bt}_{oc}")
                    o("vector", lambda e, oc=oc, ps=ps, tb=tb: e.scalar_tensor_tensor(
                        ACCN[:, tb, oc], ps[:, 0:TB], B2C[:, oc:oc+1], WB,
                        ALU.add, ALU.mult),
                      reads=[pk, "B2C", "WBb", ("ACCN", tb, oc)],
                      writes=[("ACCN", tb, oc)], tag=f"e1_{ei}_{bt}_{oc}")
                    if ei == NE - 1:
                        o("vector", lambda e, oc=oc, tb=tb: e.tensor_add(
                            ACCN[:, tb, oc], ACCN[:, tb, oc], ACC1[:, tb]),
                          reads=[("ACCN", tb, oc), ("ACC1", tb)],
                          writes=[("ACCN", tb, oc)], tag=f"e2_{bt}_{oc}")
                if ei == 0:
                    d("sync", lambda e, bt=bt, tb=tb: e.dma_start(
                        out=accs[bt].rearrange("(kc p) tt -> p kc tt", p=128),
                        in_=ACCN[:, tb]),
                      reads=[("ACCN", tb, oc) for oc in range(8)], writes=[("accs", bt)],
                      tag=f"aw{ei}_{bt}")
                elif ei < NE - 1:
                    d("gpsimd", lambda e, bt=bt, tb=tb: e.dma_start(
                        out=accs[bt].rearrange("(kc p) tt -> p kc tt", p=128),
                        in_=ACCN[:, tb], accum_op=ALU.add),
                      reads=[("ACCN", tb, oc) for oc in range(8)] + [("accs", bt)],
                      writes=[("accs", bt)], tag=f"aw{ei}_{bt}")
                else:
                    d("sync", lambda e, bt=bt, bt0=bt0: e.dma_start(
                        out=X2L, in_=x2s[bt0:bt0+TB, :].rearrange("(q p) c -> p q c", p=128)),
                      reads=[("x2s", bt), "X2Lb"], writes=["X2Lb"], tag=f"xlB{bt}")
                    for q in range(2):
                        def ftr(e, q=q, tb=tb):
                            last = None
                            for cc in range(4):
                                last = e.transpose(PS0[:, cc*128:(cc+1)*128],
                                                   ACCN[:, tb, cc, q*128:(q+1)*128], IDF)
                            for cc in range(4, 8):
                                last = e.transpose(PS1[:, (cc-4)*128:(cc-3)*128],
                                                   ACCN[:, tb, cc, q*128:(q+1)*128], IDF)
                            return last
                        o("tensor", ftr,
                          reads=[("ACCN", tb, cc) for cc in range(8)] + ["IDF", "bPS0", "bPS1"],
                          writes=["bPS0", "bPS1"], tag=f"ft{bt}_{q}")
                        o("vector", lambda e, q=q: e.tensor_add(
                            OUTST[:, q, 0:512], PS0[:, 0:512], X2L[:, q, 0:512]),
                          reads=["bPS0", "X2Lb", ("OUTST", q)], writes=[("OUTST", q)],
                          tag=f"f1{bt}_{q}")
                        o("vector", lambda e, q=q: e.tensor_add(
                            OUTST[:, q, 512:1024], PS1[:, 0:512], X2L[:, q, 512:1024]),
                          reads=["bPS1", "X2Lb", ("OUTST", q)], writes=[("OUTST", q)],
                          tag=f"f2{bt}_{q}")
                    d("sync", lambda e, bt0=bt0: e.dma_start(
                        out=out_t[bt0:bt0+TB, :].rearrange("(q p) c -> p q c", p=128),
                        in_=OUTST),
                      reads=[("OUTST", 0), ("OUTST", 1)], writes=[("out", bt)],
                      tag=f"os{bt}")

    kb.finalize_and_emit()
    return nc


_NC_CACHE = None


def kernel(**inputs):
    global _NC_CACHE
    consts = host_consts()
    if _NC_CACHE is None:
        _NC_CACHE = build_program()
    nc = _NC_CACHE
    x = np.ascontiguousarray(np.asarray(inputs["L0_attn_hidden_states"], np.float32))
    gi = lambda n: np.ascontiguousarray(np.asarray(inputs[n], np.float32))
    base = {
        "W_r": gi("L0_attn_W_r"), "W_k": gi("L0_attn_W_k"),
        "W_v": gi("L0_attn_W_v"), "W_o": gi("L0_attn_W_o"),
        "w_lora_a": gi("L0_attn_w_lora_a"), "w_lora_b": gi("L0_attn_w_lora_b"),
        "a_lora_a": gi("L0_attn_a_lora_a"), "a_lora_b": gi("L0_attn_a_lora_b"),
        "g_lora_a": gi("L0_attn_g_lora_a"), "g_lora_b": gi("L0_attn_g_lora_b"),
        "gate_w": gi("L0_ffn_gate_w"),
        "W1": gi("L0_ffn_W1"), "W2": gi("L0_ffn_W2"),
    }
    vecs = np.stack([gi("L0_attn_" + n).reshape(8, 128).T for n in VEC_NAMES], axis=-1)
    base["vecs"] = np.ascontiguousarray(vecs)
    base["r_k"] = np.ascontiguousarray(gi("L0_attn_r_k").reshape(-1).reshape(8, 128).T)
    base["b1"] = np.ascontiguousarray(gi("L0_ffn_b1").reshape(NE, 32, 128).transpose(0, 2, 1))
    base["b2"] = np.ascontiguousarray(gi("L0_ffn_b2").reshape(NE, 8, 128).transpose(0, 2, 1))
    base.update(consts)
    in_maps = [dict(base, x=np.ascontiguousarray(x[b])) for b in range(8)]
    res = run_bass_kernel_spmd(nc, in_maps, core_ids=list(range(8)))
    out = np.stack([np.asarray(r["out"], np.float32) for r in res.results], axis=0)
    if DBG:
        kernel.last_results = res.results
    return out
